# version 08 (final)
# baseline (speedup 1.0000x reference)
import numpy as np
import concourse.bass as bass
import concourse.mybir as mybir
from concourse.bass_utils import run_bass_kernel_spmd
from concourse.bass_core_rust import IndirectOffsetOnAxis

F32 = mybir.dt.float32
F32R = mybir.dt.float32r
BF16 = mybir.dt.bfloat16
I32 = mybir.dt.int32
AF = mybir.ActivationFunctionType
ALU = mybir.AluOpType

NCORE = 8
T = 16384
TC = 2048
D = 1024
DI = 2048
DFF = 4096
NE = 16
CAP = 256
LCH = 256
NCH = T // LCH
CCH = TC // LCH
KEXT = 32
_POISON = False
PI = float(np.pi)


class P:

    def __init__(self, nc):
        self.nc = nc
        self.sem = nc.alloc_semaphore()
        self.count = 0
        self._consts = {}
        self._n = 0
        nc.gpsimd.sem_clear(self.sem)
        nc.all_engine_barrier()
        self._wu = nc.alloc_sbuf_tensor("act_wu", [1, 4], F32)
        self.step(nc.vector, lambda: nc.vector.memset(self._wu[:, :], 0.25))
        for fn in (AF.Square, AF.Ln, AF.Exp, AF.Sigmoid, AF.Sin, AF.Silu, AF.Abs, AF.Copy):
            self.step(nc.scalar, lambda fn=fn: nc.scalar.activation(
                self._wu[0:1, 2:4], self._wu[0:1, 0:2], fn))

    def step(self, eng, fn, dma=False):
        nc = self.nc
        last = getattr(self, "_last_eng", None)
        if (last is not None and last is not eng and not dma
                and not getattr(self, "_in_settle", False)):
            pass
        if (last is not None and last is not eng
                and not getattr(self, "_in_settle", False)):
            self._in_settle = True
            if not hasattr(self, "_settle_ones"):
                self._settle_ones = nc.alloc_sbuf_tensor("settle_ones", [1, 1], BF16)
                self._settle_junk = nc.alloc_sbuf_tensor("settle_junk", [1, 1], F32)
                self._settle_ps = nc.alloc_psum_tensor("settle_ps", [1, 1], F32)
                self._last_eng = None
                self.step(nc.vector, lambda: nc.vector.memset(self._settle_ones[:, :], 1.0))
                self._last_eng = None
            if last is nc.tensor:
                self.step(nc.tensor, lambda: nc.tensor.matmul(
                    self._settle_ps[0:1, 0:1], self._settle_ones[0:1, 0:1],
                    self._settle_ones[0:1, 0:1], start=True, stop=True))
            elif last is nc.vector:
                self.step(nc.vector, lambda: nc.vector.memset(self._settle_junk[:, :], 0.0))
            elif last is nc.scalar:
                self.step(nc.scalar, lambda: nc.scalar.activation(
                    self._settle_junk[:, :], self._settle_ones[:, :], AF.Copy))
            self._in_settle = False
        eng.wait_ge(self.sem, self.count)
        inst = fn()
        inc = 16 if dma else 1
        inst.then_inc(self.sem, inc)
        self.count += inc
        if not getattr(self, "_in_settle", False):
            self._last_eng = eng if eng in (nc.tensor, nc.vector, nc.scalar) else None
        return inst

    def finish(self):
        self.nc.sync.wait_ge(self.sem, self.count)

    def name(self, base):
        self._n += 1
        return f"{base}_{self._n}"

    def sb(self, shape, dtype, side="left"):
        if side == "left":
            return self.nc.alloc_sbuf_tensor(self.name("sb"), shape, dtype, side="left")
        g = self.nc.sbuf_tensor(self.name("sbr"), shape, dtype, side="right")
        t = g.__enter__()
        if not hasattr(self, "_sb_stack"):
            self._sb_stack = []
        self._sb_stack.append(g)
        return t

    def ps(self, shape, dtype=F32, side="left"):
        if side == "left":
            return self.nc.alloc_psum_tensor(self.name("ps"), shape, dtype, side="left")
        g = self.nc.psum_tensor(self.name("psr"), shape, dtype, side="right")
        t = g.__enter__()
        if not hasattr(self, "_ps_stack"):
            self._ps_stack = []
        self._ps_stack.append(g)
        return t

    def mark(self):
        return (len(getattr(self, "_sb_stack", [])), len(getattr(self, "_ps_stack", [])))

    def release(self, m):
        ns, np_ = m
        while len(getattr(self, "_ps_stack", [])) > np_:
            self._ps_stack.pop().__exit__(None, None, None)
        while len(getattr(self, "_sb_stack", [])) > ns:
            self._sb_stack.pop().__exit__(None, None, None)

    def const_col(self, value, parts=128):
        key = (float(value), parts)
        if key not in self._consts:
            t = self.sb([parts, 1], F32)
            self.step(self.nc.vector, lambda: self.nc.vector.memset(t[:, :], float(value)))
            self._consts[key] = t
        return self._consts[key][:, 0:1]


def make_identities(nc, p, id_dram_ap):
    id_f = p.sb([128, 128], F32)
    id_b = p.sb([128, 128], BF16)
    ones_col_f = p.sb([128, 1], F32)
    ones_col_b = p.sb([128, 1], BF16)
    ones_row_f = p.sb([1, 128], F32)
    p.step(nc.sync, lambda: nc.sync.dma_start(out=id_f[:, :], in_=id_dram_ap), dma=True)
    p.step(nc.vector, lambda: nc.vector.tensor_copy(id_b[:, :], id_f[:, :]))
    p.step(nc.vector, lambda: nc.vector.memset(ones_col_f[:, :], 1.0))
    p.step(nc.vector, lambda: nc.vector.memset(ones_col_b[:, :], 1.0))
    p.step(nc.vector, lambda: nc.vector.memset(ones_row_f[:, :], 1.0))
    return dict(id_f=id_f, id_b=id_b, ones_col_f=ones_col_f, ones_col_b=ones_col_b,
                ones_row_f=ones_row_f)


def load_param_cols(nc, p, dram_ap, n, ident):
    out = p.sb([128, n], F32)
    m = p.mark()
    raw = p.sb([n, 128], F32, side="right")
    psd = p.ps([128, n], F32, side="right")
    junk = p.ps([1, 1], F32, side="right")
    p.step(nc.sync, lambda: nc.sync.dma_start(out=raw[:, :], in_=dram_ap), dma=True)
    p.step(nc.tensor, lambda: nc.tensor.transpose(psd[:, :], raw[:, :], ident["id_f"][0:n, 0:n]))
    p.step(nc.tensor, lambda: nc.tensor.transpose(
        junk[0:1, 0:1], ident["ones_row_f"][0:1, 0:1], ident["id_f"][0:1, 0:1]))
    p.step(nc.vector, lambda: nc.vector.tensor_copy(out[:, :], psd[:, :]))
    p.release(m)
    return out


def load_wT(nc, p, name, w_dram_ap, rows, cols, side="left"):
    nchunk = cols // 128
    wbf = nc.dram_tensor(name + "_bfd", [rows, cols], BF16, kind="Internal").ap()
    p.step(nc.gpsimd, lambda: nc.gpsimd.dma_start(out=wbf, in_=w_dram_ap), dma=True)
    tiles = []
    for c in range(nchunk):
        t = p.sb([128, rows], BF16, side=side)
        p.step(nc.sync, lambda t=t, c=c: nc.sync.dma_start_transpose(
            t[:, :], wbf[:, c * 128:(c + 1) * 128]), dma=True)
        tiles.append(t)
    return tiles


def emit_ln_feature_major(nc, p, x_tiles, ntok, ident, name="ln", out_tiles=None):
    nch = len(x_tiles)
    C = nch * 128
    ones_b = ident["ones_col_b"][:, 0:1]
    if out_tiles is None:
        out_tiles = [p.sb([128, ntok], BF16) for _ in range(nch)]
    mm_ = p.mark()
    ps_sum = p.ps([1, 512], side="right")
    ps_sq = p.ps([1, 512], side="right")
    ps_bc = p.ps([128, 512], side="right")
    ps_bc2 = p.ps([128, 512], side="right")
    hi_t = p.sb([128, 512], BF16, side="right")
    lo_t = p.sb([128, 512], BF16, side="right")
    sq_t = p.sb([128, 512], F32, side="right")
    m_row = p.sb([1, ntok], F32, side="right")
    ms_row = p.sb([1, ntok], F32, side="right")
    a_row = p.sb([1, 512], F32, side="right")
    b_row = p.sb([1, 512], F32, side="right")
    ahi = p.sb([1, 512], BF16, side="right")
    alo = p.sb([1, 512], BF16, side="right")
    bhi = p.sb([1, 512], BF16, side="right")
    blo = p.sb([1, 512], BF16, side="right")
    abc_t = p.sb([128, 512], F32, side="right")
    t_sc = p.sb([128, 512], F32, side="right")
    or_hi = p.sb([1, 128], BF16, side="right")
    p.step(nc.vector, lambda: nc.vector.memset(or_hi[:, :], 1.0))

    def split_sum(ps, src_ap, c, w, is_first, is_last):
        p.step(nc.vector, lambda: nc.vector.tensor_copy(hi_t[:, 0:w], src_ap))
        p.step(nc.vector, lambda: nc.vector.tensor_sub(lo_t[:, 0:w], src_ap, hi_t[:, 0:w]))
        p.step(nc.tensor, lambda: nc.tensor.matmul(
            ps[:, 0:w], ones_b, hi_t[:, 0:w], start=is_first, stop=False))
        p.step(nc.tensor, lambda: nc.tensor.matmul(
            ps[:, 0:w], ones_b, lo_t[:, 0:w], start=False, stop=is_last))

    slices = []
    o = 0
    while o < ntok:
        w = min(512, ntok - o)
        slices.append((o, w))
        o += w

    for (o, w) in slices:
        for c in range(nch):
            split_sum(ps_sum, x_tiles[c][:, o:o + w], c, w, c == 0, c == nch - 1)
        for c in range(nch):
            p.step(nc.scalar, lambda c=c, o=o, w=w: nc.scalar.activation(
                sq_t[:, 0:w], x_tiles[c][:, o:o + w], AF.Square))
            split_sum(ps_sq, sq_t[:, 0:w], c, w, c == 0, c == nch - 1)
        p.step(nc.vector, lambda o=o, w=w: nc.vector.tensor_single_scalar(
            m_row[:, o:o + w], ps_sum[:, 0:w], 1.0 / C, ALU.mult))
        p.step(nc.vector, lambda o=o, w=w: nc.vector.tensor_single_scalar(
            ms_row[:, o:o + w], ps_sq[:, 0:w], 1.0 / C, ALU.mult))
    for (o, w) in slices:
        p.step(nc.vector, lambda o=o, w=w: nc.vector.tensor_mul(
            a_row[:, 0:w], m_row[:, o:o + w], m_row[:, o:o + w]))
        p.step(nc.vector, lambda o=o, w=w: nc.vector.tensor_sub(
            a_row[:, 0:w], ms_row[:, o:o + w], a_row[:, 0:w]))
        p.step(nc.scalar, lambda w=w: nc.scalar.activation(
            a_row[:, 0:w], a_row[:, 0:w], AF.Ln, bias=p.const_col(1e-5, 1), scale=1.0))
        p.step(nc.scalar, lambda w=w: nc.scalar.activation(
            a_row[:, 0:w], a_row[:, 0:w], AF.Exp, bias=0.0, scale=p.const_col(-0.5, 1)))
        p.step(nc.vector, lambda o=o, w=w: nc.vector.tensor_mul(
            b_row[:, 0:w], m_row[:, o:o + w], a_row[:, 0:w]))
        p.step(nc.vector, lambda w=w: nc.vector.tensor_single_scalar(
            b_row[:, 0:w], b_row[:, 0:w], -1.0, ALU.mult))
        p.step(nc.vector, lambda w=w: nc.vector.tensor_copy(ahi[:, 0:w], a_row[:, 0:w]))
        p.step(nc.vector, lambda w=w: nc.vector.tensor_sub(alo[:, 0:w], a_row[:, 0:w], ahi[:, 0:w]))
        p.step(nc.vector, lambda w=w: nc.vector.tensor_copy(bhi[:, 0:w], b_row[:, 0:w]))
        p.step(nc.vector, lambda w=w: nc.vector.tensor_sub(blo[:, 0:w], b_row[:, 0:w], bhi[:, 0:w]))
        p.step(nc.tensor, lambda o=o, w=w: nc.tensor.matmul(
            ps_bc[:, 0:w], or_hi[0:1, :], ahi[:, 0:w], start=True, stop=False))
        p.step(nc.tensor, lambda o=o, w=w: nc.tensor.matmul(
            ps_bc[:, 0:w], or_hi[0:1, :], alo[:, 0:w], start=False, stop=True))
        p.step(nc.tensor, lambda o=o, w=w: nc.tensor.matmul(
            ps_bc2[:, 0:w], or_hi[0:1, :], bhi[:, 0:w], start=True, stop=False))
        p.step(nc.tensor, lambda o=o, w=w: nc.tensor.matmul(
            ps_bc2[:, 0:w], or_hi[0:1, :], blo[:, 0:w], start=False, stop=True))
        p.step(nc.vector, lambda w=w: nc.vector.tensor_copy(abc_t[:, 0:w], ps_bc[:, 0:w]))
        for c in range(nch):
            p.step(nc.vector, lambda c=c, o=o, w=w: nc.vector.tensor_mul(
                t_sc[:, 0:w], x_tiles[c][:, o:o + w], abc_t[:, 0:w]))
            p.step(nc.vector, lambda c=c, o=o, w=w: nc.vector.tensor_add(
                out_tiles[c][:, o:o + w], t_sc[:, 0:w], ps_bc2[:, 0:w]))
    p.release(mm_)
    return out_tiles


def build_l0a():
    nc = bass.Bass("TRN2", target_bir_lowering=False)
    p = P(nc)
    TH = TC + 128
    xh = nc.dram_tensor("xh", [D, TH], F32, kind="ExternalInput").ap()
    Wk = nc.dram_tensor("Wk", [D, D], F32, kind="ExternalInput").ap()
    Wv = nc.dram_tensor("Wv", [D, D], F32, kind="ExternalInput").ap()
    Wr = nc.dram_tensor("Wr", [D, D], F32, kind="ExternalInput").ap()
    tmix = nc.dram_tensor("tmix", [8, 128], F32, kind="ExternalInput").ap()
    td = nc.dram_tensor("td", [8, 128], F32, kind="ExternalInput").ap()
    tf = nc.dram_tensor("tf", [8, 128], F32, kind="ExternalInput").ap()
    idm = nc.dram_tensor("idm", [128, 128], F32, kind="ExternalInput").ap()

    o_wkv = nc.dram_tensor("o_wkv", [D, TC], F32, kind="ExternalOutput").ap()
    o_wk = nc.dram_tensor("o_wk", [D, TC], F32, kind="ExternalOutput").ap()
    o_sig = nc.dram_tensor("o_sig", [D, TC], F32, kind="ExternalOutput").ap()
    o_xn = nc.dram_tensor("o_xn", [D, TC], BF16, kind="ExternalOutput").ap()
    o_finkv = nc.dram_tensor("o_finkv", [D, 1], F32, kind="ExternalOutput").ap()
    o_chk = nc.dram_tensor("o_chk", [128, 24], F32, kind="ExternalOutput").ap()
    o_k0 = nc.dram_tensor("o_k0", [128, TC], F32, kind="ExternalOutput").ap()
    o_xm = nc.dram_tensor("o_xm", [D, 64], BF16, kind="ExternalOutput").ap()
    o_fink = nc.dram_tensor("o_fink", [D, 1], F32, kind="ExternalOutput").ap()

    ident = make_identities(nc, p, idm)
    tm_t = load_param_cols(nc, p, tmix, 8, ident)
    td_t = load_param_cols(nc, p, td, 8, ident)
    tf_t = load_param_cols(nc, p, tf, 8, ident)
    dec_t = p.sb([128, 8], F32)
    fw_t = p.sb([128, 8], F32)
    tm1m = p.sb([128, 8], F32)
    p.step(nc.scalar, lambda: nc.scalar.activation(dec_t[:, :], td_t[:, :], AF.Exp))
    p.step(nc.scalar, lambda: nc.scalar.activation(
        dec_t[:, :], dec_t[:, :], AF.Exp, bias=0.0, scale=p.const_col(-1.0)))
    p.step(nc.scalar, lambda: nc.scalar.activation(fw_t[:, :], tf_t[:, :], AF.Exp))
    p.step(nc.vector, lambda: nc.vector.tensor_scalar(
        tm1m[:, :], tm_t[:, :], -1.0, 1.0, ALU.mult, ALU.add))

    m_xf = p.mark()
    xf = []
    for c in range(8):
        t = p.sb([128, TH], F32, side="right")
        p.step(nc.sync, lambda t=t, c=c: nc.sync.dma_start(
            out=t[:, :], in_=xh[c * 128:(c + 1) * 128, :]), dma=True)
        xf.append(t)

    xn = emit_ln_feature_major(nc, p, xf, TH, ident, name="ln0")
    for c in range(8):
        p.step(nc.sync, lambda c=c: nc.sync.dma_start(
            out=o_xn[c * 128:(c + 1) * 128, :], in_=xn[c][:, 128:TH]), dma=True)

    xm = [p.sb([128, TC], BF16) for _ in range(8)]
    tmp_b = p.sb([128, TC], BF16, side="right")
    for c in range(8):
        p.step(nc.vector, lambda c=c: nc.vector.tensor_scalar(
            tmp_b[:, :], xn[c][:, 128:TH], tm_t[:, c:c + 1], None, ALU.mult))
        p.step(nc.vector, lambda c=c: nc.vector.scalar_tensor_tensor(
            xm[c][:, :], xn[c][:, 127:TH - 1], tm1m[:, c:c + 1], tmp_b[:, :],
            ALU.mult, ALU.add))
    p.release(m_xf)

    WkT = load_wT(nc, p, "Wk", Wk, D, D)
    WvT = load_wT(nc, p, "Wv", Wv, D, D)
    WrT = load_wT(nc, p, "Wr", Wr, D, D)

    chk = p.sb([128, 24], F32)
    for wi, WT in enumerate((WkT, WvT, WrT)):
        for c in range(8):
            p.step(nc.vector, lambda wi=wi, c=c, WT=WT: nc.vector.reduce_sum(
                chk[:, wi * 8 + c:wi * 8 + c + 1], WT[c][:, :], mybir.AxisListType.X))
    p.step(nc.sync, lambda: nc.sync.dma_start(out=o_chk, in_=chk[:, :]), dma=True)
    for c in range(8):
        p.step(nc.sync, lambda c=c: nc.sync.dma_start(
            out=o_xm[c * 128:(c + 1) * 128, :], in_=xm[c][:, 0:64]), dma=True)

    ps_a = p.ps([128, TC])
    ps_b = ps_a
    kexp = p.sb([128, TC], F32)
    kv = p.sb([128, TC], F32)
    sig = p.sb([128, TC], F32)
    s_kv = p.sb([128, TC + 1], F32)
    s_k = p.sb([128, TC + 1], F32)
    wloc = p.sb([128, TC], F32)

    def mm_weight(psd, WT, i):
        for n in range(4):
            for j in range(8):
                p.step(nc.tensor, lambda n=n, j=j: nc.tensor.matmul(
                    psd[:, n * 512:(n + 1) * 512], WT[j][:, i * 128:(i + 1) * 128],
                    xm[j][:, n * 512:(n + 1) * 512], start=(j == 0), stop=(j == 7)))

    for i in range(8):
        mm_weight(ps_a, WkT, i)
        p.step(nc.vector, lambda: nc.vector.tensor_scalar_min(kexp[:, :], ps_a[:, :], 60.0))
        if i == 0:
            p.step(nc.sync, lambda: nc.sync.dma_start(out=o_k0, in_=kexp[:, :]), dma=True)
        p.step(nc.scalar, lambda: nc.scalar.activation(kexp[:, :], kexp[:, :], AF.Exp))
        mm_weight(ps_b, WvT, i)
        p.step(nc.vector, lambda: nc.vector.tensor_mul(kv[:, :], kexp[:, :], ps_b[:, :]))
        mm_weight(ps_a, WrT, i)
        p.step(nc.scalar, lambda: nc.scalar.activation(sig[:, :], ps_a[:, :], AF.Sigmoid))
        p.step(nc.sync, lambda i=i: nc.sync.dma_start(
            out=o_sig[i * 128:(i + 1) * 128, :], in_=sig[:, :]), dma=True)
        p.step(nc.vector, lambda: nc.vector.memset(s_kv[:, 0:1], 0.0))
        p.step(nc.vector, lambda: nc.vector.memset(s_k[:, 0:1], 0.0))
        dec_b = dec_t[:, i:i + 1].broadcast_to([128, TC])
        p.step(nc.vector, lambda dec_b=dec_b: nc.vector.tensor_tensor_scan(
            s_kv[:, 1:TC + 1], dec_b, kv[:, :], 0.0, ALU.mult, ALU.add))
        p.step(nc.vector, lambda dec_b=dec_b: nc.vector.tensor_tensor_scan(
            s_k[:, 1:TC + 1], dec_b, kexp[:, :], 0.0, ALU.mult, ALU.add))
        p.step(nc.vector, lambda i=i: nc.vector.scalar_tensor_tensor(
            wloc[:, :], kv[:, :], fw_t[:, i:i + 1], s_kv[:, 0:TC], ALU.mult, ALU.add))
        p.step(nc.sync, lambda i=i: nc.sync.dma_start(
            out=o_wkv[i * 128:(i + 1) * 128, :], in_=wloc[:, :]), dma=True)
        p.step(nc.vector, lambda i=i: nc.vector.tensor_scalar(
            wloc[:, :], kexp[:, :], fw_t[:, i:i + 1], 1e-16, ALU.mult, ALU.add))
        p.step(nc.vector, lambda: nc.vector.tensor_add(wloc[:, :], wloc[:, :], s_k[:, 0:TC]))
        p.step(nc.sync, lambda i=i: nc.sync.dma_start(
            out=o_wk[i * 128:(i + 1) * 128, :], in_=wloc[:, :]), dma=True)
        p.step(nc.sync, lambda i=i: nc.sync.dma_start(
            out=o_finkv[i * 128:(i + 1) * 128, :], in_=s_kv[:, TC:TC + 1]), dma=True)
        p.step(nc.sync, lambda i=i: nc.sync.dma_start(
            out=o_fink[i * 128:(i + 1) * 128, :], in_=s_k[:, TC:TC + 1]), dma=True)
    p.finish()
    return nc


def _poison_sbuf(nc, p, val=7777.0):
    free_b = nc.sbuf_bytes_remaining - 4096
    n = free_b // 4
    m = p.mark()
    big = p.sb([128, n], F32, side="right")
    p.step(nc.vector, lambda: nc.vector.memset(big[:, :], val))
    p.release(m)


def build_l0b():
    nc = bass.Bass("TRN2", target_bir_lowering=False)
    p = P(nc)
    _POISON and _poison_sbuf(nc, p)
    xres_d = nc.dram_tensor("xres", [D, TC], F32, kind="ExternalInput").ap()
    wkv_d = nc.dram_tensor("wkv", [D, TC], F32, kind="ExternalInput").ap()
    wk_d = nc.dram_tensor("wk", [D, TC], F32, kind="ExternalInput").ap()
    sig_d = nc.dram_tensor("sig", [D, TC], F32, kind="ExternalInput").ap()
    xn_d = nc.dram_tensor("xn", [D, TC], BF16, kind="ExternalInput").ap()
    finkv_d = nc.dram_tensor("finkv", [D, 8], F32, kind="ExternalInput").ap()
    fink_d = nc.dram_tensor("fink", [D, 8], F32, kind="ExternalInput").ap()
    gaps_d = nc.dram_tensor("gaps", [1, 8], F32, kind="ExternalInput").ap()
    mask_d = nc.dram_tensor("maskv", [1, 8], F32, kind="ExternalInput").ap()
    td = nc.dram_tensor("td", [8, 128], F32, kind="ExternalInput").ap()
    Wo = nc.dram_tensor("Wo", [D, D], F32, kind="ExternalInput").ap()
    W1 = nc.dram_tensor("W1", [D, DFF], F32, kind="ExternalInput").ap()
    b1 = nc.dram_tensor("b1", [32, 128], F32, kind="ExternalInput").ap()
    th_d = nc.dram_tensor("theta", [32, 128], F32, kind="ExternalInput").ap()
    qwr_d = nc.dram_tensor("qwr", [32, 128], F32, kind="ExternalInput").ap()
    qwi_d = nc.dram_tensor("qwi", [32, 128], F32, kind="ExternalInput").ap()
    W2 = nc.dram_tensor("W2", [DFF, D], F32, kind="ExternalInput").ap()
    b2 = nc.dram_tensor("b2", [8, 128], F32, kind="ExternalInput").ap()
    idm = nc.dram_tensor("idm", [128, 128], F32, kind="ExternalInput").ap()
    iotad = nc.dram_tensor("iotad", [128, 1024], F32, kind="ExternalInput").ap()
    o_x1 = nc.dram_tensor("o_x1", [D, TC], F32, kind="ExternalOutput").ap()
    o_xa = nc.dram_tensor("o_xa", [D, TC], F32, kind="ExternalOutput").ap()
    o_rw = nc.dram_tensor("o_rw", [D, TC], BF16, kind="ExternalOutput").ap()
    o_dbg = nc.dram_tensor("o_dbg", [128, 48], F32, kind="ExternalOutput").ap()

    ident = make_identities(nc, p, idm)
    xres = [p.sb([128, TC], F32) for _ in range(8)]
    for c in range(8):
        p.step(nc.sync, lambda c=c: nc.sync.dma_start(
            out=xres[c][:, :], in_=xres_d[c * 128:(c + 1) * 128, :]), dma=True)

    td_t = load_param_cols(nc, p, td, 8, ident)
    ld_t = p.sb([128, 8], F32)
    p.step(nc.scalar, lambda: nc.scalar.activation(ld_t[:, :], td_t[:, :], AF.Exp))
    p.step(nc.vector, lambda: nc.vector.tensor_single_scalar(ld_t[:, :], ld_t[:, :], -1.0, ALU.mult))

    row8 = p.sb([1, 8], F32)
    g_bc = p.sb([128, 8], F32)
    m_bc = p.sb([128, 8], F32)
    m_ps8 = p.mark()
    ps8 = p.ps([128, 8], side="right")
    p.step(nc.sync, lambda: nc.sync.dma_start(out=row8[:, :], in_=gaps_d), dma=True)
    p.step(nc.tensor, lambda: nc.tensor.matmul(
        ps8[:, :], ident["ones_row_f"][0:1, :], row8[:, :], start=True, stop=True))
    p.step(nc.vector, lambda: nc.vector.tensor_copy(g_bc[:, :], ps8[:, :]))
    p.step(nc.sync, lambda: nc.sync.dma_start(out=row8[:, :], in_=mask_d), dma=True)
    p.step(nc.tensor, lambda: nc.tensor.matmul(
        ps8[:, :], ident["ones_row_f"][0:1, :], row8[:, :], start=True, stop=True))
    p.step(nc.vector, lambda: nc.vector.tensor_copy(m_bc[:, :], ps8[:, :]))
    p.release(m_ps8)

    fchunk = p.sb([128, 8], F32)
    coef = p.sb([128, 8], F32)
    sin_kv = p.sb([128, 8], F32)
    sin_k = p.sb([128, 8], F32)
    junk8 = p.sb([128, 8], F32)
    p.step(nc.vector, lambda: nc.vector.memset(sin_kv[:, :], 0.0))
    p.step(nc.vector, lambda: nc.vector.memset(sin_k[:, :], 0.0))
    for c in range(8):
        p.step(nc.vector, lambda c=c: nc.vector.tensor_scalar(
            coef[:, :], g_bc[:, :], ld_t[:, c:c + 1], None, ALU.mult))
        p.step(nc.scalar, lambda: nc.scalar.activation(coef[:, :], coef[:, :], AF.Exp))
        p.step(nc.vector, lambda: nc.vector.tensor_mul(coef[:, :], coef[:, :], m_bc[:, :]))
        for (fin, dst) in ((finkv_d, sin_kv), (fink_d, sin_k)):
            p.step(nc.sync, lambda fin=fin, c=c: nc.sync.dma_start(
                out=fchunk[:, :], in_=fin[c * 128:(c + 1) * 128, :]), dma=True)
            p.step(nc.vector, lambda dst=dst, c=c: nc.vector.scalar_tensor_tensor(
                junk8[:, :], coef[:, :], 1.0, fchunk[:, :], ALU.mult, ALU.mult,
                accum_out=dst[:, c:c + 1]))

    TH2 = TC // 2
    iota_t = p.sb([128, TH2], F32)
    p.step(nc.sync, lambda: nc.sync.dma_start(out=iota_t[:, :], in_=iotad), dma=True)
    h_off = p.sb([128, 1], F32)

    m_attn = p.mark()
    WoT = load_wT(nc, p, "Wo", Wo, D, D, side="right")
    rwkv = [p.sb([128, TH2], BF16, side="right") for _ in range(8)]
    dpow = p.sb([128, TH2], F32, side="right")
    tA = p.sb([128, TH2], F32, side="right")
    tB = p.sb([128, TH2], F32, side="right")
    tCt = p.sb([128, TH2], F32, side="right")
    ps_o = p.ps([128, TH2], side="right")

    p.step(nc.sync, lambda: nc.sync.dma_start(out=o_dbg[:, 0:8], in_=g_bc[:, :]), dma=True)
    p.step(nc.sync, lambda: nc.sync.dma_start(out=o_dbg[:, 8:16], in_=m_bc[:, :]), dma=True)
    p.step(nc.sync, lambda: nc.sync.dma_start(out=o_dbg[:, 16:24], in_=sin_kv[:, :]), dma=True)
    p.step(nc.sync, lambda: nc.sync.dma_start(out=o_dbg[:, 24:32], in_=sin_k[:, :]), dma=True)
    p.step(nc.sync, lambda: nc.sync.dma_start(out=o_dbg[:, 32:40], in_=ld_t[:, :]), dma=True)
    p.step(nc.sync, lambda: nc.sync.dma_start(out=o_dbg[:, 40:48], in_=iota_t[:, 0:8]), dma=True)
    for h in range(2):
        sl = slice(h * TH2, (h + 1) * TH2)
        for c in range(8):
            if h == 0:
                p.step(nc.scalar, lambda c=c: nc.scalar.activation(
                    dpow[:, :], iota_t[:, :], AF.Exp, bias=0.0, scale=ld_t[:, c:c + 1]))
            else:
                p.step(nc.scalar, lambda c=c: nc.scalar.activation(
                    dpow[:, :], iota_t[:, :], AF.Exp, bias=0.0, scale=ld_t[:, c:c + 1]))
                p.step(nc.vector, lambda c=c: nc.vector.tensor_single_scalar(
                    h_off[:, :], ld_t[:, c:c + 1], float(TH2), ALU.mult))
                p.step(nc.scalar, lambda: nc.scalar.activation(h_off[:, :], h_off[:, :], AF.Exp))
                p.step(nc.vector, lambda: nc.vector.tensor_scalar(
                    dpow[:, :], dpow[:, :], h_off[:, 0:1], None, ALU.mult))
            p.step(nc.sync, lambda c=c, sl=sl: nc.sync.dma_start(
                out=tA[:, :], in_=wkv_d[c * 128:(c + 1) * 128, sl]), dma=True)
            p.step(nc.vector, lambda c=c: nc.vector.scalar_tensor_tensor(
                tB[:, :], dpow[:, :], sin_kv[:, c:c + 1], tA[:, :], ALU.mult, ALU.add))
            p.step(nc.sync, lambda c=c, sl=sl: nc.sync.dma_start(
                out=tA[:, :], in_=wk_d[c * 128:(c + 1) * 128, sl]), dma=True)
            p.step(nc.vector, lambda c=c: nc.vector.scalar_tensor_tensor(
                tCt[:, :], dpow[:, :], sin_k[:, c:c + 1], tA[:, :], ALU.mult, ALU.add))
            p.step(nc.vector, lambda: nc.vector.reciprocal(tCt[:, :], tCt[:, :]))
            p.step(nc.sync, lambda c=c, sl=sl: nc.sync.dma_start(
                out=tA[:, :], in_=sig_d[c * 128:(c + 1) * 128, sl]), dma=True)
            p.step(nc.vector, lambda: nc.vector.tensor_mul(tB[:, :], tB[:, :], tA[:, :]))
            p.step(nc.vector, lambda c=c: nc.vector.tensor_mul(rwkv[c][:, :], tB[:, :], tCt[:, :]))
            p.step(nc.sync, lambda c=c, sl=sl: nc.sync.dma_start(
                out=o_rw[c * 128:(c + 1) * 128, sl], in_=rwkv[c][:, :]), dma=True)
        for i in range(8):
            for n in range(TH2 // 512):
                for j in range(8):
                    p.step(nc.tensor, lambda i=i, n=n, j=j: nc.tensor.matmul(
                        ps_o[:, n * 512:(n + 1) * 512], WoT[j][:, i * 128:(i + 1) * 128],
                        rwkv[j][:, n * 512:(n + 1) * 512], start=(j == 0), stop=(j == 7)))
            p.step(nc.vector, lambda i=i, sl=sl: nc.vector.tensor_add(
                xres[i][:, sl], xres[i][:, sl], ps_o[:, :]))
            p.step(nc.sync, lambda i=i, sl=sl: nc.sync.dma_start(
                out=o_xa[i * 128:(i + 1) * 128, sl], in_=xres[i][:, sl]), dma=True)
    p.release(m_attn)

    xnb = [p.sb([128, TC], BF16) for _ in range(8)]
    for c in range(8):
        p.step(nc.sync, lambda c=c: nc.sync.dma_start(
            out=xnb[c][:, :], in_=xn_d[c * 128:(c + 1) * 128, :]), dma=True)
    b1_t = load_param_cols(nc, p, b1, 32, ident)
    th_t = load_param_cols(nc, p, th_d, 32, ident)
    qwr_t = load_param_cols(nc, p, qwr_d, 32, ident)
    qwi_t = load_param_cols(nc, p, qwi_d, 32, ident)
    b2_t = load_param_cols(nc, p, b2, 8, ident)

    def emit_sin32(dst, src_ap, extra_bias):
        m_s = p.mark()
        t1 = p.sb([128, 32], F32, side="right")
        ti = p.sb([128, 32], I32, side="right")
        p.step(nc.vector, lambda: nc.vector.tensor_scalar(
            t1[:, :], src_ap, 1.0 / (2 * PI), extra_bias / (2 * PI), ALU.mult, ALU.add))
        p.step(nc.vector, lambda: nc.vector.tensor_copy(ti[:, :], t1[:, :]))
        p.step(nc.vector, lambda: nc.vector.tensor_copy(dst[:, :], ti[:, :]))
        p.step(nc.vector, lambda: nc.vector.tensor_sub(t1[:, :], t1[:, :], dst[:, :]))
        p.step(nc.scalar, lambda: nc.scalar.activation(
            dst[:, :], t1[:, :], AF.Sin, bias=0.0, scale=p.const_col(2 * PI)))
        p.release(m_s)

    sin_th = p.sb([128, 32], F32)
    cos_th = p.sb([128, 32], F32)
    A_t = p.sb([128, 32], F32)
    thpp = p.sb([128, 32], F32)
    emit_sin32(sin_th, th_t[:, :], 0.0)
    emit_sin32(cos_th, th_t[:, :], PI / 2)
    p.step(nc.vector, lambda: nc.vector.tensor_mul(A_t[:, :], cos_th[:, :], qwr_t[:, :]))
    p.step(nc.vector, lambda: nc.vector.tensor_mul(thpp[:, :], sin_th[:, :], qwi_t[:, :]))
    p.step(nc.vector, lambda: nc.vector.tensor_add(A_t[:, :], A_t[:, :], thpp[:, :]))
    p.step(nc.vector, lambda: nc.vector.tensor_mul(A_t[:, :], A_t[:, :], qwi_t[:, :]))
    p.step(nc.vector, lambda: nc.vector.tensor_single_scalar(A_t[:, :], A_t[:, :], 0.1, ALU.mult))
    p.step(nc.vector, lambda: nc.vector.tensor_scalar(
        thpp[:, :], b1_t[:, :], 0.1 / (2 * PI), None, ALU.mult))
    p.step(nc.vector, lambda: nc.vector.scalar_tensor_tensor(
        thpp[:, :], th_t[:, :], 1.0 / (2 * PI), thpp[:, :], ALU.mult, ALU.add))

    W1bf = nc.dram_tensor("W1_bfd", [D, DFF], BF16, kind="Internal").ap()
    W2bf = nc.dram_tensor("W2_bfd", [DFF, D], BF16, kind="Internal").ap()
    p.step(nc.gpsimd, lambda: nc.gpsimd.dma_start(out=W1bf, in_=W1), dma=True)
    p.step(nc.gpsimd, lambda: nc.gpsimd.dma_start(out=W2bf, in_=W2), dma=True)

    TQ = 1024
    act = [p.sb([128, TQ], BF16) for _ in range(32)]
    wblk = p.sb([128, 128], BF16)
    ps_h = p.ps([128, TQ])
    f_t = p.sb([128, TQ], F32)
    f_i = p.sb([128, TQ], I32)
    f_t2 = p.sb([128, TQ], F32)
    s_t = p.sb([128, TQ], F32)

    for h in range(2):
        sl = slice(h * TQ, (h + 1) * TQ)
        for i in range(32):
            for j in range(8):
                p.step(nc.sync, lambda i=i, j=j: nc.sync.dma_start(
                    out=wblk[:, :], in_=W1bf[j * 128:(j + 1) * 128, i * 128:(i + 1) * 128]),
                    dma=True)
                for n in range(TQ // 512):
                    p.step(nc.tensor, lambda i=i, n=n, j=j, sl=sl: nc.tensor.matmul(
                        ps_h[:, n * 512:(n + 1) * 512], wblk[:, :],
                        xnb[j][:, sl.start + n * 512:sl.start + (n + 1) * 512],
                        start=(j == 0), stop=(j == 7)))
            p.step(nc.vector, lambda i=i: nc.vector.tensor_scalar(
                f_t[:, :], ps_h[:, :], 0.1 / (2 * PI), thpp[:, i:i + 1], ALU.mult, ALU.add))
            p.step(nc.vector, lambda: nc.vector.tensor_copy(f_i[:, :], f_t[:, :]))
            p.step(nc.vector, lambda: nc.vector.tensor_copy(f_t2[:, :], f_i[:, :]))
            p.step(nc.vector, lambda: nc.vector.tensor_sub(f_t[:, :], f_t[:, :], f_t2[:, :]))
            p.step(nc.scalar, lambda: nc.scalar.activation(
                s_t[:, :], f_t[:, :], AF.Sin, bias=0.0, scale=p.const_col(2 * PI)))
            p.step(nc.vector, lambda i=i: nc.vector.scalar_tensor_tensor(
                f_t[:, :], s_t[:, :], A_t[:, i:i + 1], ps_h[:, :], ALU.mult, ALU.add))
            p.step(nc.vector, lambda i=i: nc.vector.tensor_scalar(
                act[i][:, :], f_t[:, :], b1_t[:, i:i + 1], 0.0, ALU.add, ALU.max))
        for d in range(8):
            for i in range(32):
                p.step(nc.sync, lambda d=d, i=i: nc.sync.dma_start(
                    out=wblk[:, :], in_=W2bf[i * 128:(i + 1) * 128, d * 128:(d + 1) * 128]),
                    dma=True)
                for n in range(TQ // 512):
                    p.step(nc.tensor, lambda d=d, i=i, n=n: nc.tensor.matmul(
                        ps_h[:, n * 512:(n + 1) * 512], wblk[:, :],
                        act[i][:, n * 512:(n + 1) * 512], start=(i == 0), stop=(i == 31)))
            p.step(nc.vector, lambda d=d: nc.vector.tensor_scalar(
                f_t[:, :], ps_h[:, :], 1.0, b2_t[:, d:d + 1], ALU.mult, ALU.add))
            p.step(nc.vector, lambda d=d, sl=sl: nc.vector.tensor_add(
                xres[d][:, sl], xres[d][:, sl], f_t[:, :]))
    for c in range(8):
        p.step(nc.sync, lambda c=c: nc.sync.dma_start(
            out=o_x1[c * 128:(c + 1) * 128, :], in_=xres[c][:, :]), dma=True)
    p.finish()
    return nc



def build_l1a():
    nc = bass.Bass("TRN2", target_bir_lowering=False)
    p = P(nc)
    x1_d = nc.dram_tensor("x1", [D, TC], F32, kind="ExternalInput").ap()
    Win = nc.dram_tensor("Win", [2 * DI, D], F32, kind="ExternalInput").ap()
    Wl = nc.dram_tensor("Wl", [DI, DI], F32, kind="ExternalInput").ap()
    Wv1 = nc.dram_tensor("Wv1", [DI, DI], F32, kind="ExternalInput").ap()
    bl = nc.dram_tensor("bl", [16, 128], F32, kind="ExternalInput").ap()
    bv = nc.dram_tensor("bv", [16, 128], F32, kind="ExternalInput").ap()
    om = nc.dram_tensor("om", [16, 128], F32, kind="ExternalInput").ap()
    rw1 = nc.dram_tensor("rw1", [D, 128], F32, kind="ExternalInput").ap()
    rw2 = nc.dram_tensor("rw2", [128, 16], F32, kind="ExternalInput").ap()
    fm = nc.dram_tensor("fm", [128, 1], F32, kind="ExternalInput").ap()
    idm = nc.dram_tensor("idm", [128, 128], F32, kind="ExternalInput").ap()
    mup = nc.dram_tensor("mup", [256, 256], F32, kind="ExternalInput").ap()
    mlo = nc.dram_tensor("mlo", [256, 256], F32, kind="ExternalInput").ap()

    o_xn1f = nc.dram_tensor("o_xn1f", [D, TC], BF16, kind="ExternalOutput").ap()
    o_xn1t = nc.dram_tensor("o_xn1t", [TC, D], BF16, kind="ExternalOutput").ap()
    o_logit = nc.dram_tensor("o_logit", [16, TC], F32, kind="ExternalOutput").ap()
    o_vt = nc.dram_tensor("o_vt", [CCH * DI, LCH], BF16, kind="ExternalOutput").ap()
    o_gt = nc.dram_tensor("o_gt", [CCH * 256, 256], BF16, kind="ExternalOutput").ap()
    o_b32 = nc.dram_tensor("o_b32", [CCH * 256, 128], BF16, kind="ExternalOutput").ap()
    msks = nc.dram_tensor("msks", [8 * 128, 128], F32, kind="ExternalInput").ap()
    o_kp = nc.dram_tensor("o_kp", [CCH * 256, DI], BF16, kind="ExternalOutput").ap()
    o_dpre = nc.dram_tensor("o_dpre", [CCH, 256], F32, kind="ExternalOutput").ap()
    o_qpre = nc.dram_tensor("o_qpre", [CCH * 128, 16], F32, kind="ExternalOutput").ap()
    o_laml = nc.dram_tensor("o_laml", [CCH * 128, 16], F32, kind="ExternalOutput").ap()
    o_xi0 = nc.dram_tensor("o_xi0", [128, 16], F32, kind="ExternalOutput").ap()
    o_lam = nc.dram_tensor("o_lam", [DI, TC], F32, kind="ExternalOutput").ap()
    o_s = nc.dram_tensor("o_s", [DI, TC], F32, kind="ExternalOutput").ap()
    o_m = nc.dram_tensor("o_m", [DI, TC], BF16, kind="ExternalOutput").ap()
    o_c = nc.dram_tensor("o_c", [DI, TC], BF16, kind="ExternalOutput").ap()
    o_sz = nc.dram_tensor("o_sz", [DI, TC], BF16, kind="ExternalOutput").ap()

    xn1d = nc.dram_tensor("xn1d", [D, TC], BF16, kind="Internal").ap()

    ident = make_identities(nc, p, idm)
    ones_row_b = p.sb([1, 128], BF16)
    p.step(nc.vector, lambda: nc.vector.tensor_copy(ones_row_b[:, :], ident["ones_row_f"][:, :]))
    bl_t = load_param_cols(nc, p, bl, 16, ident)
    bv_t = load_param_cols(nc, p, bv, 16, ident)
    om_t = load_param_cols(nc, p, om, 16, ident)
    fm_t = p.sb([128, 1], F32)
    p.step(nc.sync, lambda: nc.sync.dma_start(out=fm_t[:, :], in_=fm), dma=True)
    s_c = p.sb([128, 16], F32)
    t16 = p.sb([128, 16], F32)
    p.step(nc.scalar, lambda: nc.scalar.activation(t16[:, :], om_t[:, :], AF.Abs))
    p.step(nc.scalar, lambda: nc.scalar.activation(
        t16[:, :], t16[:, :], AF.Exp, bias=0.0, scale=p.const_col(-1.0)))
    p.step(nc.scalar, lambda: nc.scalar.activation(
        t16[:, :], t16[:, :], AF.Ln, bias=p.const_col(1.0), scale=1.0))
    p.step(nc.vector, lambda: nc.vector.tensor_scalar_max(s_c[:, :], om_t[:, :], 0.0))
    p.step(nc.vector, lambda: nc.vector.tensor_add(s_c[:, :], s_c[:, :], t16[:, :]))
    p.step(nc.vector, lambda: nc.vector.tensor_single_scalar(s_c[:, :], s_c[:, :], -8.0, ALU.mult))

    xi = [p.sb([128, TC], BF16) for _ in range(16)]
    xi0_sb = p.sb([128, 16], F32)

    mXN = p.mark()
    xn1 = [p.sb([128, TC], BF16, side="right") for _ in range(8)]
    mA = p.mark()
    xres = []
    for c in range(8):
        t = p.sb([128, TC], F32, side="right")
        p.step(nc.sync, lambda t=t, c=c: nc.sync.dma_start(
            out=t[:, :], in_=x1_d[c * 128:(c + 1) * 128, :]), dma=True)
        xres.append(t)
    emit_ln_feature_major(nc, p, xres, TC, ident, name="ln1", out_tiles=xn1)
    p.release(mA)
    for c in range(8):
        p.step(nc.sync, lambda c=c: nc.sync.dma_start(
            out=o_xn1f[c * 128:(c + 1) * 128, :], in_=xn1[c][:, :]), dma=True)
        p.step(nc.sync, lambda c=c: nc.sync.dma_start(
            out=xn1d[c * 128:(c + 1) * 128, :], in_=xn1[c][:, :]), dma=True)
    mB0 = p.mark()
    tok_t = p.sb([128, D], BF16, side="right")
    for tb in range(16):
        p.step(nc.sync, lambda tb=tb: nc.sync.dma_start_transpose(
            tok_t[:, :], xn1d[:, tb * 128:(tb + 1) * 128]), dma=True)
        p.step(nc.sync, lambda tb=tb: nc.sync.dma_start(
            out=o_xn1t[tb * 128:(tb + 1) * 128, :], in_=tok_t[:, :]), dma=True)
    p.release(mB0)

    mR = p.mark()
    rw1b = [p.sb([128, 128], BF16, side="right") for _ in range(8)]
    rw2b = p.sb([128, 16], BF16, side="right")
    h_r = p.sb([128, TC], BF16, side="right")
    lg = p.sb([16, TC], F32, side="right")
    ps_r = p.ps([128, TC], side="right")
    ps_l = p.ps([16, TC // 2], side="right")
    for j in range(8):
        p.step(nc.gpsimd, lambda j=j: nc.gpsimd.dma_start(
            out=rw1b[j][:, :], in_=rw1[j * 128:(j + 1) * 128, :]), dma=True)
    p.step(nc.gpsimd, lambda: nc.gpsimd.dma_start(out=rw2b[:, :], in_=rw2), dma=True)
    for n in range(4):
        for j in range(8):
            p.step(nc.tensor, lambda n=n, j=j: nc.tensor.matmul(
                ps_r[:, n * 512:(n + 1) * 512], rw1b[j][:, :],
                xn1[j][:, n * 512:(n + 1) * 512], start=(j == 0), stop=(j == 7)))
    p.step(nc.scalar, lambda: nc.scalar.activation(h_r[:, :], ps_r[:, :], AF.Silu))
    for hh in range(2):
        for n in range(2):
            p.step(nc.tensor, lambda hh=hh, n=n: nc.tensor.matmul(
                ps_l[:, n * 512:(n + 1) * 512], rw2b[:, :],
                h_r[:, hh * 1024 + n * 512:hh * 1024 + (n + 1) * 512], start=True, stop=True))
        p.step(nc.vector, lambda hh=hh: nc.vector.tensor_copy(
            lg[:, hh * 1024:(hh + 1) * 1024], ps_l[:, :]))
    p.step(nc.sync, lambda: nc.sync.dma_start(out=o_logit, in_=lg[:, :]), dma=True)
    p.release(mR)

    mW = p.mark()
    WinT = load_wT(nc, p, "Win", Win, 2 * DI, D, side="right")
    ps_m = p.ps([128, TC], side="right")
    szt = p.sb([128, TC], BF16, side="right")
    for i in range(32):
        for n in range(4):
            for j in range(8):
                p.step(nc.tensor, lambda i=i, n=n, j=j: nc.tensor.matmul(
                    ps_m[:, n * 512:(n + 1) * 512], WinT[j][:, i * 128:(i + 1) * 128],
                    xn1[j][:, n * 512:(n + 1) * 512], start=(j == 0), stop=(j == 7)))
        if i < 16:
            p.step(nc.vector, lambda i=i: nc.vector.tensor_copy(xi[i][:, :], ps_m[:, :]))
            p.step(nc.vector, lambda i=i: nc.vector.tensor_copy(
                xi0_sb[:, i:i + 1], ps_m[:, 0:1]))
        else:
            p.step(nc.scalar, lambda: nc.scalar.activation(szt[:, :], ps_m[:, :], AF.Silu))
            p.step(nc.sync, lambda i=i: nc.sync.dma_start(
                out=o_sz[(i - 16) * 128:(i - 15) * 128, :], in_=szt[:, :]), dma=True)
    p.step(nc.sync, lambda: nc.sync.dma_start(out=o_xi0, in_=xi0_sb[:, :]), dma=True)
    p.release(mW)
    p.release(mXN)

    mP1 = p.mark()
    WlT = load_wT(nc, p, "Wl", Wl, DI, DI, side="right")
    ps_m2 = p.ps([128, TC], side="right")
    lam_t = p.sb([128, TC], F32, side="right")
    s_t = p.sb([128, TC], F32, side="right")
    for i in range(16):
        for n in range(4):
            for j in range(16):
                p.step(nc.tensor, lambda i=i, n=n, j=j: nc.tensor.matmul(
                    ps_m2[:, n * 512:(n + 1) * 512], WlT[j][:, i * 128:(i + 1) * 128],
                    xi[j][:, n * 512:(n + 1) * 512], start=(j == 0), stop=(j == 15)))
        p.step(nc.scalar, lambda i=i: nc.scalar.activation(
            lam_t[:, :], ps_m2[:, :], AF.Sigmoid, bias=bl_t[:, i:i + 1], scale=1.0))
        p.step(nc.vector, lambda i=i: nc.vector.tensor_scalar(
            lam_t[:, :], lam_t[:, :], s_c[:, i:i + 1], None, ALU.mult))
        p.step(nc.vector, lambda: nc.vector.tensor_mul(lam_t[:, 0:1], lam_t[:, 0:1], fm_t[:, :]))
        for k in range(CCH):
            p.step(nc.vector, lambda k=k: nc.vector.tensor_tensor_scan(
                s_t[:, k * LCH:(k + 1) * LCH],
                ident["ones_col_f"][:, 0:1].broadcast_to([128, LCH]),
                lam_t[:, k * LCH:(k + 1) * LCH], 0.0, ALU.mult, ALU.add))
        p.step(nc.sync, lambda i=i: nc.sync.dma_start(
            out=o_s[i * 128:(i + 1) * 128, :], in_=s_t[:, :]), dma=True)
        p.step(nc.scalar, lambda: nc.scalar.activation(lam_t[:, :], lam_t[:, :], AF.Exp))
        p.step(nc.sync, lambda i=i: nc.sync.dma_start(
            out=o_lam[i * 128:(i + 1) * 128, :], in_=lam_t[:, :]), dma=True)
    p.release(mP1)

    mP2 = p.mark()
    WvT = load_wT(nc, p, "Wv1", Wv1, DI, DI, side="right")
    ps_v = p.ps([128, LCH], side="right")
    ps_vv = p.ps([1, LCH], side="right")
    ps_g = p.ps([128, LCH], side="right")
    ps_g2 = p.ps([128, LCH], side="right")
    ps_row = p.ps([1, LCH], side="right")
    ps_tp = p.ps([128, 128], BF16, side="right")
    ps_tm = p.ps([128, 128], F32, side="right")
    v_bf = [p.sb([128, LCH], BF16, side="right") for _ in range(16)]
    vt_b = [p.sb([128, LCH], BF16, side="right") for _ in range(16)]
    mt_b = [p.sb([128, LCH], BF16, side="right") for _ in range(16)]
    ct_b = [p.sb([128, LCH], BF16, side="right") for _ in range(16)]
    v_f = p.sb([128, LCH], F32, side="right")
    w_t = p.sb([128, LCH], F32, side="right")
    ep_t = p.sb([128, LCH + 1], F32, side="right")
    em_t = p.sb([128, LCH + 1], F32, side="right")
    lam_c = p.sb([128, LCH], F32, side="right")
    s_c2 = p.sb([128, LCH], F32, side="right")
    v2r = p.sb([128, LCH], BF16, side="right")
    v2r2 = p.sb([128, LCH], BF16, side="right")
    wvrow = p.sb([1, LCH], F32, side="right")
    w2vv = p.sb([128, LCH], F32, side="right")
    cf32 = p.sb([128, LCH], F32, side="right")
    qp_t = p.sb([128, 16], F32, side="right")
    ll_t = p.sb([128, 16], F32, side="right")
    gt_b = [p.sb([128, LCH], BF16, side="right") for _ in range(2)]
    gl_b = [p.sb([128, LCH], BF16, side="right") for _ in range(2)]
    du_b = p.sb([128, LCH], BF16, side="right")
    mup_t = [p.sb([128, LCH], F32, side="right") for _ in range(2)]
    mlo_t = [p.sb([128, LCH], F32, side="right") for _ in range(2)]
    dpre_r = p.sb([1, LCH], F32, side="right")
    lb = {nm: p.sb([128, 128], BF16, side="right") for nm in
          ("P", "PT", "B", "BT", "tmp", "tmp2", "B11", "B11T", "B22", "tblk")}
    gblk_b = p.sb([128, 128], BF16, side="right")
    gtblk_b = p.sb([128, 128], BF16, side="right")
    msk_t = [p.sb([128, 128], F32, side="right") for _ in range(8)]
    for mi in range(8):
        p.step(nc.sync, lambda mi=mi: nc.sync.dma_start(
            out=msk_t[mi][:, :], in_=msks[mi * 128:(mi + 1) * 128, :]), dma=True)
    for h in range(2):
        p.step(nc.sync, lambda h=h: nc.sync.dma_start(
            out=mup_t[h][:, :], in_=mup[h * 128:(h + 1) * 128, :]), dma=True)
        p.step(nc.sync, lambda h=h: nc.sync.dma_start(
            out=mlo_t[h][:, :], in_=mlo[h * 128:(h + 1) * 128, :]), dma=True)

    def stable_binv(Gblk, Gtblk, Bout, BTout):
        X, XT = lb["B"], lb["BT"]
        A, AT = lb["P"], lb["PT"]
        T1, T2 = lb["tmp"], lb["tmp2"]
        p.step(nc.vector, lambda: nc.vector.tensor_mul(A[:, :], Gblk, msk_t[0][:, :]))
        p.step(nc.vector, lambda: nc.vector.tensor_mul(AT[:, :], Gtblk, msk_t[1][:, :]))
        p.step(nc.tensor, lambda: nc.tensor.matmul(ps_tm[:, :], AT[:, :], A[:, :], start=True, stop=True))
        p.step(nc.vector, lambda: nc.vector.tensor_copy(T1[:, :], ps_tm[:, :]))
        p.step(nc.tensor, lambda: nc.tensor.matmul(ps_tm[:, :], A[:, :], AT[:, :], start=True, stop=True))
        p.step(nc.vector, lambda: nc.vector.tensor_copy(T2[:, :], ps_tm[:, :]))
        p.step(nc.vector, lambda: nc.vector.tensor_sub(X[:, :], ident["id_b"][:, :], A[:, :]))
        p.step(nc.vector, lambda: nc.vector.tensor_add(X[:, :], X[:, :], T1[:, :]))
        p.step(nc.vector, lambda: nc.vector.tensor_sub(XT[:, :], ident["id_b"][:, :], AT[:, :]))
        p.step(nc.vector, lambda: nc.vector.tensor_add(XT[:, :], XT[:, :], T2[:, :]))
        p.step(nc.tensor, lambda: nc.tensor.matmul(ps_tm[:, :], T2[:, :], A[:, :], start=True, stop=True))
        p.step(nc.vector, lambda: nc.vector.tensor_copy(lb["B11"][:, :], ps_tm[:, :]))
        p.step(nc.tensor, lambda: nc.tensor.matmul(ps_tm[:, :], T1[:, :], AT[:, :], start=True, stop=True))
        p.step(nc.vector, lambda: nc.vector.tensor_copy(lb["B11T"][:, :], ps_tm[:, :]))
        p.step(nc.vector, lambda: nc.vector.tensor_sub(X[:, :], X[:, :], lb["B11"][:, :]))
        p.step(nc.vector, lambda: nc.vector.tensor_sub(XT[:, :], XT[:, :], lb["B11T"][:, :]))
        for mi in (2, 4, 6):
            p.step(nc.vector, lambda mi=mi: nc.vector.tensor_mul(A[:, :], Gblk, msk_t[mi][:, :]))
            p.step(nc.vector, lambda mi=mi: nc.vector.tensor_mul(AT[:, :], Gtblk, msk_t[mi + 1][:, :]))
            p.step(nc.tensor, lambda: nc.tensor.matmul(ps_tm[:, :], AT[:, :], X[:, :], start=True, stop=True))
            p.step(nc.vector, lambda: nc.vector.tensor_copy(T1[:, :], ps_tm[:, :]))
            p.step(nc.tensor, lambda: nc.tensor.matmul(ps_tm[:, :], XT[:, :], T1[:, :], start=True, stop=True))
            p.step(nc.vector, lambda: nc.vector.tensor_copy(T2[:, :], ps_tm[:, :]))
            p.step(nc.vector, lambda: nc.vector.tensor_sub(X[:, :], X[:, :], T2[:, :]))
            p.step(nc.tensor, lambda: nc.tensor.matmul(ps_tm[:, :], A[:, :], XT[:, :], start=True, stop=True))
            p.step(nc.vector, lambda: nc.vector.tensor_copy(T1[:, :], ps_tm[:, :]))
            p.step(nc.tensor, lambda: nc.tensor.matmul(ps_tm[:, :], X[:, :], T1[:, :], start=True, stop=True))
            p.step(nc.vector, lambda: nc.vector.tensor_copy(T2[:, :], ps_tm[:, :]))
            p.step(nc.vector, lambda: nc.vector.tensor_sub(XT[:, :], XT[:, :], T2[:, :]))
        p.step(nc.vector, lambda: nc.vector.tensor_copy(Bout, X[:, :]))
        p.step(nc.vector, lambda: nc.vector.tensor_copy(BTout, XT[:, :]))

    for k in range(CCH):
        ksl = slice(k * LCH, (k + 1) * LCH)
        for i in range(16):
            for j in range(16):
                p.step(nc.tensor, lambda i=i, j=j, ksl=ksl: nc.tensor.matmul(
                    ps_v[:, :], WvT[j][:, i * 128:(i + 1) * 128],
                    xi[j][:, ksl], start=(j == 0), stop=(j == 15)))
            p.step(nc.vector, lambda i=i: nc.vector.tensor_scalar(
                v_bf[i][:, :], ps_v[:, :], 1.0, bv_t[:, i:i + 1], ALU.mult, ALU.add))
            p.step(nc.scalar, lambda i=i: nc.scalar.activation(
                cf32[:, :], ps_v[:, :], AF.Square, bias=bv_t[:, i:i + 1], scale=1.0))
            p.step(nc.vector, lambda: nc.vector.tensor_copy(v2r[:, :], cf32[:, :]))
            p.step(nc.vector, lambda: nc.vector.tensor_sub(v2r2[:, :], cf32[:, :], v2r[:, :]))
            p.step(nc.tensor, lambda i=i: nc.tensor.matmul(
                ps_vv[:, :], ident["ones_col_b"][:, 0:1], v2r[:, :],
                start=(i == 0), stop=False))
            p.step(nc.tensor, lambda i=i: nc.tensor.matmul(
                ps_vv[:, :], ident["ones_col_b"][:, 0:1], v2r2[:, :],
                start=False, stop=(i == 15)))
        p.step(nc.vector, lambda: nc.vector.reciprocal(wvrow[:, :], ps_vv[:, :]))
        p.step(nc.vector, lambda: nc.vector.tensor_single_scalar(wvrow[:, :], wvrow[:, :], 2.0, ALU.mult))
        p.step(nc.vector, lambda: nc.vector.tensor_copy(v2r[0:1, :], wvrow[:, :]))
        p.step(nc.tensor, lambda: nc.tensor.matmul(
            ps_g[:, :], ones_row_b[0:1, :], v2r[0:1, :], start=True, stop=True))
        p.step(nc.vector, lambda: nc.vector.tensor_copy(w2vv[:, :], ps_g[:, :]))
        p.step(nc.vector, lambda: nc.vector.memset(qp_t[:, :], 0.0))
        for i in range(16):
            p.step(nc.sync, lambda i=i, ksl=ksl: nc.sync.dma_start(
                out=lam_c[:, :], in_=o_lam[i * 128:(i + 1) * 128, ksl]), dma=True)
            p.step(nc.sync, lambda i=i, ksl=ksl: nc.sync.dma_start(
                out=s_c2[:, :], in_=o_s[i * 128:(i + 1) * 128, ksl]), dma=True)
            p.step(nc.vector, lambda: nc.vector.memset(ep_t[:, 0:1], 1.0))
            p.step(nc.vector, lambda: nc.vector.memset(em_t[:, 0:1], 1.0))
            p.step(nc.scalar, lambda: nc.scalar.activation(
                ep_t[:, 1:LCH + 1], s_c2[:, :], AF.Exp))
            p.step(nc.scalar, lambda: nc.scalar.activation(
                em_t[:, 1:LCH + 1], s_c2[:, :], AF.Exp, bias=0.0, scale=p.const_col(-1.0)))
            p.step(nc.vector, lambda i=i: nc.vector.tensor_mul(
                vt_b[i][:, :], v_bf[i][:, :], ep_t[:, 0:LCH]))
            if k == 0:
                p.step(nc.vector, lambda i=i: nc.vector.tensor_mul(
                    vt_b[i][:, 0:1], vt_b[i][:, 0:1], fm_t[:, :]))
            p.step(nc.vector, lambda i=i: nc.vector.tensor_mul(
                w_t[:, :], v_bf[i][:, :], w2vv[:, :]))
            p.step(nc.vector, lambda i=i: nc.vector.tensor_mul(
                mt_b[i][:, :], w_t[:, :], em_t[:, 0:LCH]))
            p.step(nc.vector, lambda i=i: nc.vector.tensor_mul(
                cf32[:, :], w_t[:, :], lam_c[:, :]))
            p.step(nc.gpsimd, lambda i=i, ksl=ksl: nc.gpsimd.dma_start(
                out=o_m[i * 128:(i + 1) * 128, ksl], in_=cf32[:, :]), dma=True)
            p.step(nc.vector, lambda: nc.vector.tensor_scalar(
                s_c2[:, :], lam_c[:, :], -1.0, 1.0, ALU.mult, ALU.add))
            p.step(nc.vector, lambda i=i, ksl=ksl: nc.vector.tensor_mul(
                cf32[:, :], xi[i][:, ksl], s_c2[:, :]))
            p.step(nc.gpsimd, lambda i=i, ksl=ksl: nc.gpsimd.dma_start(
                out=o_c[i * 128:(i + 1) * 128, ksl], in_=cf32[:, :]), dma=True)
            p.step(nc.vector, lambda i=i: nc.vector.scalar_tensor_tensor(
                ct_b[i][:, :], cf32[:, :], 1.0, em_t[:, 1:LCH + 1],
                ALU.mult, ALU.mult, accum_out=qp_t[:, i:i + 1]))
            p.step(nc.vector, lambda i=i: nc.vector.tensor_copy(
                ll_t[:, i:i + 1], ep_t[:, LCH:LCH + 1]))
            p.step(nc.sync, lambda i=i, k=k: nc.sync.dma_start(
                out=o_vt[k * DI + i * 128:k * DI + (i + 1) * 128, :], in_=vt_b[i][:, :]), dma=True)
        p.step(nc.vector, lambda: nc.vector.tensor_mul(qp_t[:, :], qp_t[:, :], ll_t[:, :]))
        p.step(nc.sync, lambda k=k: nc.sync.dma_start(
            out=o_qpre[k * 128:(k + 1) * 128, :], in_=qp_t[:, :]), dma=True)
        p.step(nc.sync, lambda k=k: nc.sync.dma_start(
            out=o_laml[k * 128:(k + 1) * 128, :], in_=ll_t[:, :]), dma=True)
        for i in range(16):
            p.step(nc.vector, lambda i=i: nc.vector.tensor_scalar(
                du_b[:, :], mt_b[i][:, :], ll_t[:, i:i + 1], None, ALU.mult))
            for h in range(2):
                p.step(nc.tensor, lambda i=i, h=h: nc.tensor.transpose(
                    ps_tp[:, :], du_b[:, h * 128:(h + 1) * 128], ident["id_b"][:, :]))
                p.step(nc.vector, lambda: nc.vector.tensor_copy(gt_b[0][:, 0:128], ps_tp[:, :]))
                p.step(nc.sync, lambda i=i, h=h, k=k: nc.sync.dma_start(
                    out=o_kp[k * 256 + h * 128:k * 256 + (h + 1) * 128,
                             i * 128:(i + 1) * 128], in_=gt_b[0][:, 0:128]), dma=True)
        for h in range(2):
            for i in range(16):
                p.step(nc.tensor, lambda h=h, i=i: nc.tensor.matmul(
                    ps_g[:, :], mt_b[i][:, h * 128:(h + 1) * 128], vt_b[i][:, :],
                    start=(i == 0), stop=(i == 15)))
                p.step(nc.tensor, lambda h=h, i=i: nc.tensor.matmul(
                    ps_g2[:, :], vt_b[i][:, h * 128:(h + 1) * 128], mt_b[i][:, :],
                    start=(i == 0), stop=(i == 15)))
            p.step(nc.vector, lambda h=h: nc.vector.tensor_mul(
                cf32[:, :], ps_g[:, :], mup_t[h][:, :]))
            p.step(nc.vector, lambda h=h: nc.vector.tensor_single_scalar(
                gt_b[h][:, :], cf32[:, :], -1.0, ALU.mult))
            p.step(nc.vector, lambda h=h: nc.vector.tensor_mul(
                cf32[:, :], ps_g2[:, :], mlo_t[h][:, :]))
            p.step(nc.vector, lambda h=h: nc.vector.tensor_single_scalar(
                gl_b[h][:, :], cf32[:, :], -1.0, ALU.mult))
        for h in range(2):
            for i in range(16):
                p.step(nc.tensor, lambda h=h, i=i: nc.tensor.matmul(
                    ps_g[:, :], ct_b[i][:, h * 128:(h + 1) * 128], vt_b[i][:, :],
                    start=(i == 0), stop=(i == 15)))
            p.step(nc.vector, lambda h=h: nc.vector.tensor_mul(cf32[:, :], ps_g[:, :], mup_t[h][:, :]))
            p.step(nc.vector, lambda h=h: nc.vector.tensor_copy(du_b[:, :], cf32[:, :]))
            p.step(nc.tensor, lambda h=h: nc.tensor.matmul(
                ps_row[:, :], ident["ones_col_b"][:, 0:1], du_b[:, :],
                start=(h == 0), stop=(h == 1)))
        p.step(nc.vector, lambda: nc.vector.tensor_copy(dpre_r[:, :], ps_row[:, :]))
        p.step(nc.sync, lambda k=k: nc.sync.dma_start(
            out=o_dpre[k:k + 1, :], in_=dpre_r[:, :]), dma=True)
        for h in range(2):
            p.step(nc.vector, lambda h=h: nc.vector.tensor_single_scalar(
                du_b[:, :], gt_b[h][:, :], -1.0, ALU.mult))
            p.step(nc.sync, lambda h=h, k=k: nc.sync.dma_start(
                out=o_gt[k * 256 + h * 128:k * 256 + (h + 1) * 128, :], in_=du_b[:, :]), dma=True)
        for h in range(2):
            p.step(nc.vector, lambda h=h: nc.vector.tensor_single_scalar(
                gblk_b[:, :], gt_b[h][:, h * 128:(h + 1) * 128], -1.0, ALU.mult))
            p.step(nc.vector, lambda h=h: nc.vector.tensor_single_scalar(
                gtblk_b[:, :], gl_b[h][:, h * 128:(h + 1) * 128], -1.0, ALU.mult))
            stable_binv(gblk_b[:, :], gtblk_b[:, :], lb["tblk"][:, :], lb["B22"][:, :])
            p.step(nc.sync, lambda h=h, k=k: nc.sync.dma_start(
                out=o_b32[k * 256 + h * 128:k * 256 + (h + 1) * 128, :],
                in_=lb["tblk"][:, :]), dma=True)
    p.release(mP2)
    p.finish()
    return nc


def build_l1w():
    nc = bass.Bass("TRN2", target_bir_lowering=False)
    p = P(nc)
    vt_all = nc.dram_tensor("vt_all", [NCH * DI, LCH], BF16, kind="ExternalInput").ap()
    kp_all = nc.dram_tensor("kp_all", [NCH * 256, DI], BF16, kind="ExternalInput").ap()
    gt_all = nc.dram_tensor("gt_all", [NCH * 256, 256], BF16, kind="ExternalInput").ap()
    b32_all = nc.dram_tensor("b32_all", [NCH * 256, 128], BF16, kind="ExternalInput").ap()
    dpre_all = nc.dram_tensor("dpre_all", [NCH, 256], F32, kind="ExternalInput").ap()
    qpre_all = nc.dram_tensor("qpre_all", [NCH * 128, 16], F32, kind="ExternalInput").ap()
    laml_all = nc.dram_tensor("laml_all", [NCH * 128, 16], F32, kind="ExternalInput").ap()
    xi0 = nc.dram_tensor("xi0", [128, 16], F32, kind="ExternalInput").ap()
    idm = nc.dram_tensor("idm", [128, 128], F32, kind="ExternalInput").ap()
    o_alpha = nc.dram_tensor("o_alpha", [NCH, 256], F32, kind="ExternalOutput").ap()
    o_h0 = nc.dram_tensor("o_h0", [NCH * 128, 16], F32, kind="ExternalOutput").ap()

    ident = make_identities(nc, p, idm)
    h0 = p.sb([128, 16], F32)
    h0b = p.sb([128, 16], BF16)
    vt_sb = p.sb([128, 16 * LCH], BF16)
    kp_sb = p.sb([128, 2 * DI], BF16)
    gt_sb = p.sb([128, 2 * 256], BF16)
    b32_sb = p.sb([128, 2 * 128], BF16)
    acol = p.sb([128, 4], BF16)
    t128 = p.sb([1, 128], BF16)
    trow = p.sb([1, 32], F32)
    tcol = p.sb([128, 1], BF16)
    arow_acc = p.sb([1, 256], F32)
    ps_x = None
    dpre_sb = p.sb([1, 256], F32)
    qpre_sb = p.sb([128, 16], F32)
    laml_sb = p.sb([128, 16], F32)
    d_bf = p.sb([1, 256], BF16)
    dcol = p.sb([128, 4], BF16)
    acol = p.sb([128, 4], BF16)
    arow = p.sb([1, 256], F32)
    corr_row = p.sb([1, DI], F32)
    corr_c = p.sb([128, 16], F32)
    t16a = p.sb([128, 16], F32)
    ps_d = p.ps([1, 256], F32)
    ps_a2 = p.ps([1, 256], F32)
    ps_cb = p.ps([128, 4], BF16)
    ps_c4 = p.ps([1, 512], F32)
    ps_cc = p.ps([128, 16], F32)

    p.step(nc.sync, lambda: nc.sync.dma_start(out=h0[:, :], in_=xi0), dma=True)
    for k in range(NCH):
        p.step(nc.sync, lambda k=k: nc.sync.dma_start(
            out=vt_sb[:, :].rearrange("p (i t) -> p i t", i=16),
            in_=vt_all[k * DI:(k + 1) * DI, :].rearrange("(i p) t -> p i t", p=128)), dma=True)
        p.step(nc.sync, lambda k=k: nc.sync.dma_start(
            out=kp_sb[:, :].rearrange("p (h c) -> p h c", h=2),
            in_=kp_all[k * 256:(k + 1) * 256, :].rearrange("(h p) c -> p h c", p=128)), dma=True)
        p.step(nc.sync, lambda k=k: nc.sync.dma_start(
            out=gt_sb[:, :].rearrange("p (h c) -> p h c", h=2),
            in_=gt_all[k * 256:(k + 1) * 256, :].rearrange("(h p) c -> p h c", p=128)), dma=True)
        p.step(nc.sync, lambda k=k: nc.sync.dma_start(
            out=b32_sb[:, :].rearrange("p (h c) -> p h c", h=2),
            in_=b32_all[k * 256:(k + 1) * 256, :].rearrange("(h p) c -> p h c", p=128)), dma=True)
        p.step(nc.sync, lambda k=k: nc.sync.dma_start(
            out=dpre_sb[:, :], in_=dpre_all[k:k + 1, :]), dma=True)
        p.step(nc.sync, lambda k=k: nc.sync.dma_start(
            out=qpre_sb[:, :], in_=qpre_all[k * 128:(k + 1) * 128, :]), dma=True)
        p.step(nc.sync, lambda k=k: nc.sync.dma_start(
            out=laml_sb[:, :], in_=laml_all[k * 128:(k + 1) * 128, :]), dma=True)
        p.step(nc.sync, lambda k=k: nc.sync.dma_start(
            out=o_h0[k * 128:(k + 1) * 128, :], in_=h0[:, :]), dma=True)
        p.step(nc.vector, lambda: nc.vector.tensor_copy(h0b[:, :], h0[:, :]))
        for i in range(16):
            p.step(nc.tensor, lambda i=i: nc.tensor.matmul(
                ps_d[:, :], h0b[:, i:i + 1], vt_sb[:, i * LCH:(i + 1) * LCH],
                start=(i == 0), stop=(i == 15)))
        p.step(nc.vector, lambda: nc.vector.tensor_add(arow[:, :], ps_d[:, :], dpre_sb[:, :]))
        p.step(nc.vector, lambda: nc.vector.memset(acol[:, :], 0.0))
        p.step(nc.vector, lambda: nc.vector.memset(arow_acc[:, :], 0.0))
        for b in range(8):
            h = b // 4
            bo = (b % 4) * 32
            for hh in range(2):
                p.step(nc.tensor, lambda hh=hh, h=h, bo=bo: nc.tensor.matmul(
                    ps_a2[:, 0:32], acol[:, 2 * hh:2 * hh + 1],
                    gt_sb[:, hh * 256 + h * 128 + bo:hh * 256 + h * 128 + bo + 32],
                    start=(hh == 0), stop=(hh == 1)))
            p.step(nc.vector, lambda h=h, bo=bo: nc.vector.tensor_sub(
                trow[:, :], arow[0:1, h * 128 + bo:h * 128 + bo + 32], ps_a2[:, 0:32]))
            p.step(nc.vector, lambda: nc.vector.memset(t128[:, :], 0.0))
            p.step(nc.vector, lambda bo=bo: nc.vector.tensor_copy(
                t128[0:1, bo:bo + 32], trow[:, :]))
            p.step(nc.tensor, lambda: nc.tensor.transpose(
                ps_cb[:, 0:1], t128[0:1, :], ident["id_b"][0:1, 0:1]))
            p.step(nc.vector, lambda: nc.vector.tensor_copy(tcol[:, :], ps_cb[:, 0:1]))
            p.step(nc.tensor, lambda h=h: nc.tensor.matmul(
                ps_a2[:, 0:128], tcol[:, 0:1], b32_sb[:, h * 128:(h + 1) * 128],
                start=True, stop=True))
            p.step(nc.vector, lambda h=h, bo=bo: nc.vector.tensor_copy(
                arow_acc[0:1, h * 128 + bo:h * 128 + bo + 32], ps_a2[0:1, bo:bo + 32]))
            p.step(nc.vector, lambda: nc.vector.memset(t128[:, :], 0.0))
            p.step(nc.vector, lambda bo=bo: nc.vector.tensor_copy(
                t128[0:1, bo:bo + 32], ps_a2[0:1, bo:bo + 32]))
            p.step(nc.tensor, lambda: nc.tensor.transpose(
                ps_cb[:, 0:1], t128[0:1, :], ident["id_b"][0:1, 0:1]))
            p.step(nc.vector, lambda h=h: nc.vector.tensor_add(
                acol[:, 2 * h:2 * h + 1], acol[:, 2 * h:2 * h + 1], ps_cb[:, 0:1]))
        p.step(nc.sync, lambda k=k: nc.sync.dma_start(
            out=o_alpha[k:k + 1, :], in_=arow_acc[:, :]), dma=True)
        for n in range(4):
            for h in range(2):
                p.step(nc.tensor, lambda h=h, n=n: nc.tensor.matmul(
                    ps_c4[:, :], acol[:, 2 * h:2 * h + 1],
                    kp_sb[:, h * DI + n * 512:h * DI + (n + 1) * 512],
                    start=(h == 0), stop=(h == 1)))
            p.step(nc.vector, lambda n=n: nc.vector.tensor_copy(
                corr_row[:, n * 512:(n + 1) * 512], ps_c4[:, :]))
        for i in range(16):
            p.step(nc.tensor, lambda i=i: nc.tensor.transpose(
                ps_cc[:, i:i + 1], corr_row[0:1, i * 128:(i + 1) * 128],
                ident["id_f"][0:1, 0:1]))
        p.step(nc.vector, lambda: nc.vector.tensor_copy(corr_c[:, :], ps_cc[:, :]))
        p.step(nc.vector, lambda: nc.vector.tensor_mul(t16a[:, :], laml_sb[:, :], h0[:, :]))
        p.step(nc.vector, lambda: nc.vector.tensor_sub(t16a[:, :], t16a[:, :], corr_c[:, :]))
        p.step(nc.vector, lambda: nc.vector.tensor_add(h0[:, :], t16a[:, :], qpre_sb[:, :]))
    p.finish()
    return nc


def build_l1c():
    nc = bass.Bass("TRN2", target_bir_lowering=False)
    p = P(nc)
    x1_d = nc.dram_tensor("x1", [D, TC], F32, kind="ExternalInput").ap()
    lam_d = nc.dram_tensor("lam", [DI, TC], F32, kind="ExternalInput").ap()
    m_d = nc.dram_tensor("m", [DI, TC], BF16, kind="ExternalInput").ap()
    c_d = nc.dram_tensor("c", [DI, TC], BF16, kind="ExternalInput").ap()
    sz_d = nc.dram_tensor("sz", [DI, TC], BF16, kind="ExternalInput").ap()
    al_d = nc.dram_tensor("al", [CCH, 256], F32, kind="ExternalInput").ap()
    h0_d = nc.dram_tensor("h0", [CCH * 128, 16], F32, kind="ExternalInput").ap()
    Wout = nc.dram_tensor("Wout", [D, DI], F32, kind="ExternalInput").ap()
    idm = nc.dram_tensor("idm", [128, 128], F32, kind="ExternalInput").ap()
    o_x2 = nc.dram_tensor("o_x2", [D, TC], F32, kind="ExternalOutput").ap()

    ident = make_identities(nc, p, idm)
    ones_row_b = p.sb([1, 128], BF16)
    p.step(nc.vector, lambda: nc.vector.tensor_copy(ones_row_b[:, :], ident["ones_row_f"][:, :]))
    WoT = load_wT(nc, p, "Wout", Wout, D, DI)
    xres = [p.sb([128, TC], F32) for _ in range(8)]
    for c in range(8):
        p.step(nc.sync, lambda c=c: nc.sync.dma_start(
            out=xres[c][:, :], in_=x1_d[c * 128:(c + 1) * 128, :]), dma=True)
    y_b = [p.sb([128, LCH], BF16) for _ in range(16)]
    lam_c = p.sb([128, LCH], F32)
    m_c = p.sb([128, LCH], BF16)
    c_c = p.sb([128, LCH], BF16)
    sz_c = p.sb([128, LCH], BF16)
    h0_sb = p.sb([128, 16], F32)
    ar_f = p.sb([1, 256], F32)
    ar_b = p.sb([1, 256], BF16)
    ab_t = p.sb([128, LCH], BF16)
    u_t = p.sb([128, LCH], F32)
    h_t = p.sb([128, LCH], F32)
    ps_bc = p.ps([128, LCH], F32)
    ps_o = p.ps([128, LCH], F32)

    for k in range(CCH):
        ksl = slice(k * LCH, (k + 1) * LCH)
        p.step(nc.sync, lambda k=k: nc.sync.dma_start(out=ar_f[:, :], in_=al_d[k:k + 1, :]), dma=True)
        p.step(nc.vector, lambda: nc.vector.tensor_copy(ar_b[:, :], ar_f[:, :]))
        p.step(nc.tensor, lambda: nc.tensor.matmul(
            ps_bc[:, :], ones_row_b[0:1, :], ar_b[:, :], start=True, stop=True))
        p.step(nc.vector, lambda: nc.vector.tensor_copy(ab_t[:, :], ps_bc[:, :]))
        p.step(nc.sync, lambda k=k: nc.sync.dma_start(
            out=h0_sb[:, :], in_=h0_d[k * 128:(k + 1) * 128, :]), dma=True)
        for i in range(16):
            p.step(nc.sync, lambda i=i, ksl=ksl: nc.sync.dma_start(
                out=lam_c[:, :], in_=lam_d[i * 128:(i + 1) * 128, ksl]), dma=True)
            p.step(nc.sync, lambda i=i, ksl=ksl: nc.sync.dma_start(
                out=m_c[:, :], in_=m_d[i * 128:(i + 1) * 128, ksl]), dma=True)
            p.step(nc.sync, lambda i=i, ksl=ksl: nc.sync.dma_start(
                out=c_c[:, :], in_=c_d[i * 128:(i + 1) * 128, ksl]), dma=True)
            p.step(nc.sync, lambda i=i, ksl=ksl: nc.sync.dma_start(
                out=sz_c[:, :], in_=sz_d[i * 128:(i + 1) * 128, ksl]), dma=True)
            p.step(nc.vector, lambda: nc.vector.tensor_mul(u_t[:, :], ab_t[:, :], m_c[:, :]))
            p.step(nc.vector, lambda: nc.vector.tensor_sub(u_t[:, :], c_c[:, :], u_t[:, :]))
            p.step(nc.vector, lambda i=i: nc.vector.tensor_tensor_scan(
                h_t[:, :], lam_c[:, :], u_t[:, :], h0_sb[:, i:i + 1], ALU.mult, ALU.add))
            p.step(nc.vector, lambda i=i: nc.vector.tensor_mul(y_b[i][:, :], h_t[:, :], sz_c[:, :]))
        for d in range(8):
            for i in range(16):
                p.step(nc.tensor, lambda d=d, i=i: nc.tensor.matmul(
                    ps_o[:, :], WoT[i][:, d * 128:(d + 1) * 128], y_b[i][:, :],
                    start=(i == 0), stop=(i == 15)))
            p.step(nc.vector, lambda d=d, ksl=ksl: nc.vector.tensor_add(
                xres[d][:, ksl], xres[d][:, ksl], ps_o[:, :]))
    for c in range(8):
        p.step(nc.sync, lambda c=c: nc.sync.dma_start(
            out=o_x2[c * 128:(c + 1) * 128, :], in_=xres[c][:, :]), dma=True)
    p.finish()
    return nc


def run_l1_attn(inputs, x1_tiles):
    idm = np.eye(128, dtype=np.float32)
    mup = np.zeros((256, 256), np.float32)
    for j in range(256):
        mup[j, j + 1:] = 1.0
    mlo = np.ascontiguousarray(mup.T)
    def _bd_strict(s):
        m = np.zeros((128, 128), np.float32)
        for b in range(128 // s):
            for j in range(s):
                m[b * s + j, b * s + j + 1:(b + 1) * s] = 1.0
        return m

    def _quarter(s2):
        s = s2 // 2
        m = np.zeros((128, 128), np.float32)
        for b in range(128 // s2):
            m[b * s2:b * s2 + s, b * s2 + s:(b + 1) * s2] = 1.0
        return m
    msks = np.concatenate([
        _bd_strict(4), _bd_strict(4).T,
        _quarter(8), _quarter(8).T,
        _quarter(16), _quarter(16).T,
        _quarter(32), _quarter(32).T,
    ], axis=0).astype(np.float32)
    in3 = []
    for c in range(NCORE):
        in3.append(dict(
            x1=x1_tiles[c],
            Win=np.asarray(inputs["L1_attn_Win"], np.float32),
            Wl=np.asarray(inputs["L1_attn_Wl"], np.float32),
            Wv1=np.asarray(inputs["L1_attn_Wv"], np.float32),
            bl=_r128(inputs["L1_attn_bl"], 16),
            bv=_r128(inputs["L1_attn_bv"], 16),
            om=_r128(inputs["L1_attn_omega"], 16),
            rw1=np.asarray(inputs["L1_ffn_router_w1"], np.float32),
            rw2=np.asarray(inputs["L1_ffn_router_w2"], np.float32),
            fm=np.full((128, 1), 0.0 if c == 0 else 1.0, np.float32),
            idm=idm, mup=mup, mlo=mlo, msks=msks,
        ))
    r3 = _run("l1a", build_l1a, in3)

    vt_all = np.concatenate([np.asarray(r3[c]["o_vt"]) for c in range(NCORE)], axis=0)
    kp_all = np.concatenate([np.asarray(r3[c]["o_kp"]) for c in range(NCORE)], axis=0)
    gt_all = np.concatenate([np.asarray(r3[c]["o_gt"]) for c in range(NCORE)], axis=0)
    b32_all = np.concatenate([np.asarray(r3[c]["o_b32"]) for c in range(NCORE)], axis=0)
    dpre_all = np.concatenate([np.asarray(r3[c]["o_dpre"]) for c in range(NCORE)], axis=0)
    qpre_all = np.concatenate([np.asarray(r3[c]["o_qpre"]) for c in range(NCORE)], axis=0)
    laml_all = np.concatenate([np.asarray(r3[c]["o_laml"]) for c in range(NCORE)], axis=0)
    in4 = [dict(vt_all=vt_all, kp_all=kp_all, gt_all=gt_all, b32_all=b32_all,
                dpre_all=dpre_all, qpre_all=qpre_all, laml_all=laml_all,
                xi0=np.asarray(r3[0]["o_xi0"]), idm=idm) for _ in range(NCORE)]
    r4 = _run("l1w", build_l1w, in4)
    alpha = np.asarray(r4[0]["o_alpha"])
    h0_all = np.asarray(r4[0]["o_h0"])

    in5 = []
    for c in range(NCORE):
        in5.append(dict(
            x1=x1_tiles[c], lam=np.asarray(r3[c]["o_lam"]), m=np.asarray(r3[c]["o_m"]),
            c=np.asarray(r3[c]["o_c"]), sz=np.asarray(r3[c]["o_sz"]),
            al=np.ascontiguousarray(alpha[c * CCH:(c + 1) * CCH]),
            h0=np.ascontiguousarray(h0_all[c * CCH * 128:(c + 1) * CCH * 128]),
            Wout=np.asarray(inputs["L1_attn_Wout"], np.float32), idm=idm,
        ))
    r5 = _run("l1c", build_l1c, in5)
    x2 = [np.asarray(r5[c]["o_x2"]) for c in range(NCORE)]
    xn1t = np.concatenate([np.asarray(r3[c]["o_xn1t"]) for c in range(NCORE)], axis=0)
    logits = np.concatenate([np.asarray(r3[c]["o_logit"]) for c in range(NCORE)], axis=1)
    return x2, xn1t, logits, r3



def build_moe():
    nc = bass.Bass("TRN2", target_bir_lowering=False)
    p = P(nc)
    lg_d = nc.dram_tensor("lg", [128, 128], F32, kind="ExternalInput").ap()
    xnt_d = nc.dram_tensor("xnt", [T, D], BF16, kind="ExternalInput").ap()
    w1_d = nc.dram_tensor("w1e", [D, 2048], F32, kind="ExternalInput").ap()
    w2_d = nc.dram_tensor("w2e", [D, 2048], F32, kind="ExternalInput").ap()
    w3_d = nc.dram_tensor("w3e", [2048, D], F32, kind="ExternalInput").ap()
    ltm_d = nc.dram_tensor("ltm", [128, 128], F32, kind="ExternalInput").ap()
    tv_d = nc.dram_tensor("tv", [128, 128], F32, kind="ExternalInput").ap()
    idm = nc.dram_tensor("idm", [128, 128], F32, kind="ExternalInput").ap()
    o_pair = nc.dram_tensor("o_pair", [CAP, 2], F32, kind="ExternalOutput").ap()
    o_wout = nc.dram_tensor("o_wout", [CAP, D], F32, kind="ExternalOutput").ap()
    pair_i = nc.dram_tensor("pair_i", [CAP, 2], F32, kind="Internal").ap()

    ident = make_identities(nc, p, idm)
    ones_row_b = p.sb([1, 128], BF16)
    p.step(nc.vector, lambda: nc.vector.tensor_copy(ones_row_b[:, :], ident["ones_row_f"][:, :]))
    ltm_b = p.sb([128, 128], BF16)
    p.step(nc.gpsimd, lambda: nc.gpsimd.dma_start(out=ltm_b[:, :], in_=ltm_d), dma=True)
    tv_t = p.sb([128, 128], F32)
    p.step(nc.sync, lambda: nc.sync.dma_start(out=tv_t[:, :], in_=tv_d), dma=True)
    big_t = p.sb([128, 128], F32)
    p.step(nc.vector, lambda: nc.vector.memset(big_t[:, :], 1e9))

    l_t = p.sb([128, 128], F32)
    q_t = p.sb([128, 128], F32)
    rmax = p.sb([128, 1], F32)
    row128 = p.sb([1, 128], F32)
    g11 = p.sb([1, 1], F32)
    nmax = p.sb([128, 1], F32)
    hi_c = p.sb([128, 1], F32)
    lo_c = p.sb([128, 1], F32)
    mid_c = p.sb([128, 1], F32)
    cnt_c = p.sb([128, 1], F32)
    cnt_b = p.sb([128, 1], BF16)
    tot = p.sb([1, 1], F32)
    tot_b = p.sb([1, 1], BF16)
    sel11 = p.sb([1, 1], F32)
    sel11b = p.sb([1, 1], BF16)
    selcol = p.sb([128, 1], F32)
    junk = p.sb([128, 128], F32)
    selA = p.sb([128, 128], F32)
    band = p.sb([128, 128], F32)
    band_b = p.sb([128, 128], BF16)
    rowb = p.sb([1, 128], BF16)
    brank = p.sb([128, 128], F32)
    selF = p.sb([128, 128], F32)
    incl = p.sb([128, 128], F32)
    rksel = p.sb([128, 128], F32)
    rowoff = p.sb([128, 1], F32)
    n1 = p.sb([1, 1], F32)
    rem_c = p.sb([128, 1], F32)
    pairs = p.sb([128, 64], F32)
    slots = p.sb([128, 32], F32)
    slot_i = p.sb([128, 32], I32)
    vcol = p.sb([128, 1], F32)
    scol = p.sb([128, 1], F32)
    selcol_i = p.sb([128, 1], I32)
    selF_i = p.sb([128, 128], I32)
    vcol_i = p.sb([128, 1], I32)
    zz = p.sb([128, 4], F32)
    idxf = p.sb([128, 1], F32)
    idxi = [p.sb([128, 1], I32) for _ in range(2)]
    xe_tok = [p.sb([128, D], BF16) for _ in range(2)]
    xe_f = [p.sb([128, 2 * 128], BF16) for _ in range(8)]
    w_row = p.sb([1, CAP], F32)
    wn_b = p.sb([1, CAP], BF16)
    w_bc = p.sb([128, CAP], F32)
    r11 = p.sb([1, 1], F32)
    blk = p.sb([128, 128], BF16)
    g_b = p.sb([128, CAP], BF16)
    gv = [p.sb([128, CAP], BF16) for _ in range(16)]
    wt_f = p.sb([128, CAP], F32)
    wout_tok = [p.sb([128, D], F32) for _ in range(2)]
    ps_tp = p.ps([128, 128], BF16)
    ps_tf = p.ps([128, 128], F32)
    ps_s = p.ps([128, 128], F32)
    ps11 = ps_s
    ps_c1 = ps_s
    ps_r1 = ps_s
    ps_bb = p.ps([128, 128], F32)
    ps_g1 = p.ps([128, CAP], F32)
    ps_g2 = p.ps([128, CAP], F32)
    p.step(nc.vector, lambda: nc.vector.memset(zz[:, :], 0.0))

    ones1b = ident["ones_col_b"][:, 0:1]

    def bcast11(src11b, dst_col_f32):
        p.step(nc.tensor, lambda: nc.tensor.matmul(
            ps_c1[:, 0:1], ones_row_b[0:1, :], src11b[0:1, 0:1], start=True, stop=True))
        p.step(nc.vector, lambda: nc.vector.tensor_copy(dst_col_f32[:, :], ps_c1[:, 0:1]))

    for e in range(1):
        erow = e * CAP
        p.step(nc.sync, lambda e=e: nc.sync.dma_start(
            out=l_t[:, :], in_=lg_d[e * 128:(e + 1) * 128, :]), dma=True)
        p.step(nc.vector, lambda: nc.vector.reduce_max(rmax[:, :], l_t[:, :], mybir.AxisListType.X))
        p.step(nc.tensor, lambda: nc.tensor.transpose(
            ps_r1[0:1, :], rmax[:, :], ident["id_f"][:, :]))
        p.step(nc.vector, lambda: nc.vector.tensor_copy(row128[:, :], ps_r1[0:1, :]))
        p.step(nc.vector, lambda: nc.vector.reduce_max(g11[:, :], row128[:, :], mybir.AxisListType.X))
        p.step(nc.vector, lambda: nc.vector.tensor_single_scalar(g11[:, :], g11[:, :], -1.0, ALU.mult))
        p.step(nc.vector, lambda: nc.vector.tensor_copy(sel11b[:, :], g11[:, :]))
        bcast11(sel11b, nmax)
        p.step(nc.scalar, lambda: nc.scalar.activation(
            q_t[:, :], l_t[:, :], AF.Exp, bias=nmax[:, 0:1], scale=1.0))
        p.step(nc.vector, lambda: nc.vector.memset(hi_c[:, :], 1.0001))
        p.step(nc.vector, lambda: nc.vector.memset(lo_c[:, :], 0.0))
        for it in range(30):
            p.step(nc.vector, lambda: nc.vector.tensor_add(mid_c[:, :], hi_c[:, :], lo_c[:, :]))
            p.step(nc.vector, lambda: nc.vector.tensor_single_scalar(mid_c[:, :], mid_c[:, :], 0.5, ALU.mult))
            p.step(nc.vector, lambda: nc.vector.memset(cnt_c[:, :], 0.0))
            p.step(nc.vector, lambda: nc.vector.tensor_scalar(
                junk[:, :], q_t[:, :], mid_c[:, 0:1], 1.0, ALU.is_gt, ALU.mult,
                accum_out=cnt_c[:, :]))
            p.step(nc.vector, lambda: nc.vector.tensor_copy(cnt_b[:, :], cnt_c[:, :]))
            p.step(nc.tensor, lambda: nc.tensor.matmul(
                ps11[0:1, 0:1], ones1b, cnt_b[:, :], start=True, stop=True))
            p.step(nc.vector, lambda: nc.vector.tensor_single_scalar(
                sel11[:, :], ps11[0:1, 0:1], float(CAP) + 0.5, ALU.is_le))
            p.step(nc.vector, lambda: nc.vector.tensor_copy(sel11b[:, :], sel11[:, :]))
            bcast11(sel11b, selcol)
            p.step(nc.vector, lambda: nc.vector.tensor_copy(selcol_i[:, :], selcol[:, :]))
            p.step(nc.vector, lambda: nc.vector.select(hi_c[:, :], selcol_i[:, :], mid_c[:, :], hi_c[:, :]))
            p.step(nc.vector, lambda: nc.vector.select(lo_c[:, :], selcol_i[:, :], lo_c[:, :], mid_c[:, :]))
        p.step(nc.vector, lambda: nc.vector.tensor_scalar(
            selA[:, :], q_t[:, :], hi_c[:, 0:1], None, ALU.is_gt))
        p.step(nc.vector, lambda: nc.vector.tensor_scalar(
            band[:, :], q_t[:, :], lo_c[:, 0:1], None, ALU.is_gt))
        p.step(nc.vector, lambda: nc.vector.tensor_sub(band[:, :], band[:, :], selA[:, :]))
        p.step(nc.vector, lambda: nc.vector.memset(cnt_c[:, :], 0.0))
        p.step(nc.vector, lambda: nc.vector.tensor_scalar(
            junk[:, :], selA[:, :], 1.0, 0.0, ALU.mult, ALU.add, accum_out=cnt_c[:, :]))
        p.step(nc.vector, lambda: nc.vector.tensor_copy(cnt_b[:, :], cnt_c[:, :]))
        p.step(nc.tensor, lambda: nc.tensor.matmul(
            ps11[0:1, 0:1], ones1b, cnt_b[:, :], start=True, stop=True))
        p.step(nc.vector, lambda: nc.vector.tensor_copy(n1[:, :], ps11[0:1, 0:1]))
        p.step(nc.vector, lambda: nc.vector.tensor_scalar(
            sel11[:, :], n1[:, :], -1.0, float(CAP), ALU.mult, ALU.add))
        p.step(nc.vector, lambda: nc.vector.tensor_copy(sel11b[:, :], sel11[:, :]))
        bcast11(sel11b, rem_c)
        p.step(nc.vector, lambda: nc.vector.tensor_copy(band_b[:, :], band[:, :]))
        p.step(nc.tensor, lambda: nc.tensor.matmul(
            ps_r1[0:1, :], ones1b, band_b[:, :], start=True, stop=True))
        p.step(nc.vector, lambda: nc.vector.tensor_copy(row128[:, :], ps_r1[0:1, :]))
        p.step(nc.vector, lambda: nc.vector.tensor_tensor_scan(
            row128[:, :], ident["ones_col_f"][0:1, 0:1].broadcast_to([1, 128]),
            row128[:, :], 0.0, ALU.mult, ALU.add))
        p.step(nc.vector, lambda: nc.vector.tensor_sub(row128[:, :], row128[:, :], ps_r1[0:1, :]))
        p.step(nc.vector, lambda: nc.vector.tensor_copy(rowb[:, :], row128[:, :]))
        p.step(nc.tensor, lambda: nc.tensor.matmul(
            ps_bb[:, :], ones_row_b[0:1, :], rowb[:, :], start=True, stop=True))
        p.step(nc.tensor, lambda: nc.tensor.matmul(
            ps_bb[:, :], ltm_b[:, :], band_b[:, :], start=False, stop=True))
        p.step(nc.vector, lambda: nc.vector.tensor_copy(brank[:, :], ps_bb[:, :]))
        p.step(nc.vector, lambda: nc.vector.tensor_scalar(
            junk[:, :], brank[:, :], rem_c[:, 0:1], None, ALU.is_lt))
        p.step(nc.vector, lambda: nc.vector.tensor_mul(junk[:, :], junk[:, :], band[:, :]))
        p.step(nc.vector, lambda: nc.vector.tensor_add(selF[:, :], selA[:, :], junk[:, :]))
        p.step(nc.vector, lambda: nc.vector.memset(cnt_c[:, :], 0.0))
        p.step(nc.vector, lambda: nc.vector.tensor_scalar(
            junk[:, :], selF[:, :], 1.0, 0.0, ALU.mult, ALU.add, accum_out=cnt_c[:, :]))
        p.step(nc.vector, lambda: nc.vector.tensor_tensor_scan(
            incl[:, :], ident["ones_col_f"][:, 0:1].broadcast_to([128, 128]),
            selF[:, :], 0.0, ALU.mult, ALU.add))
        p.step(nc.vector, lambda: nc.vector.tensor_sub(incl[:, :], incl[:, :], selF[:, :]))
        p.step(nc.vector, lambda: nc.vector.tensor_copy(cnt_b[:, :], cnt_c[:, :]))
        p.step(nc.tensor, lambda: nc.tensor.matmul(
            ps_c1[:, 0:1], ltm_b[:, :], cnt_b[:, :], start=True, stop=True))
        p.step(nc.vector, lambda: nc.vector.tensor_copy(rowoff[:, :], ps_c1[:, 0:1]))
        p.step(nc.vector, lambda: nc.vector.tensor_copy(selF_i[:, :], selF[:, :]))
        p.step(nc.vector, lambda: nc.vector.select(rksel[:, :], selF_i[:, :], incl[:, :], big_t[:, :]))
        for jp in range(KEXT):
            p.step(nc.vector, lambda jp=jp: nc.vector.tensor_single_scalar(
                junk[:, :], rksel[:, :], float(jp), ALU.is_equal))
            p.step(nc.vector, lambda jp=jp: nc.vector.memset(pairs[:, 2 * jp:2 * jp + 2], 0.0))
            p.step(nc.vector, lambda jp=jp: nc.vector.scalar_tensor_tensor(
                selA[:, :] if False else band[:, :], junk[:, :], 1.0, tv_t[:, :],
                ALU.mult, ALU.mult, accum_out=pairs[:, 2 * jp:2 * jp + 1]))
            p.step(nc.vector, lambda jp=jp: nc.vector.scalar_tensor_tensor(
                band[:, :], junk[:, :], 1.0, q_t[:, :],
                ALU.mult, ALU.mult, accum_out=pairs[:, 2 * jp + 1:2 * jp + 2]))
            p.step(nc.vector, lambda jp=jp: nc.vector.tensor_single_scalar(
                vcol[:, :], cnt_c[:, :], float(jp), ALU.is_gt))
            p.step(nc.vector, lambda jp=jp: nc.vector.tensor_single_scalar(
                scol[:, :], rowoff[:, :], float(jp), ALU.add))
            p.step(nc.vector, lambda jp=jp: nc.vector.tensor_copy(vcol_i[:, :], vcol[:, :]))
            p.step(nc.vector, lambda jp=jp: nc.vector.select(
                slots[:, jp:jp + 1], vcol_i[:, :], scol[:, :], big_t[:, 0:1]))
        p.step(nc.vector, lambda: nc.vector.tensor_copy(slot_i[:, :], slots[:, :]))
        p.step(nc.sync, lambda e=e: nc.sync.dma_start(
            out=pair_i[erow:erow + CAP, :].rearrange("(a b) c -> a (b c)", b=2),
            in_=zz[:, :]), dma=True)
        for jp in range(KEXT):
            p.step(nc.gpsimd, lambda jp=jp, erow=erow: nc.gpsimd.indirect_dma_start(
                pair_i[:, :],
                IndirectOffsetOnAxis(slot_i[:, jp:jp + 1], axis=0),
                pairs[:, 2 * jp:2 * jp + 2], None,
                element_offset=erow * 2,
                bounds_check=CAP - 1, oob_is_err=False), dma=True)
        for g in range(2):
            with nc.allow_non_contiguous_dma(reason="tiny 128-elem strided index column"):
                p.step(nc.sync, lambda g=g, erow=erow: nc.sync.dma_start(
                    out=idxf[:, :], in_=pair_i[erow + g * 128:erow + (g + 1) * 128, 0:1]), dma=True)
            p.step(nc.vector, lambda g=g: nc.vector.tensor_copy(idxi[g][:, :], idxf[:, :]))
            p.step(nc.gpsimd, lambda g=g: nc.gpsimd.indirect_dma_start(
                xe_tok[g][:, :], None, xnt_d,
                IndirectOffsetOnAxis(idxi[g][:, 0:1], axis=0)), dma=True)
        with nc.allow_non_contiguous_dma(reason="tiny 256-elem strided weight row"):
            p.step(nc.sync, lambda erow=erow: nc.sync.dma_start(
                out=w_row[:, :], in_=pair_i[erow:erow + CAP, 1:2].rearrange("a b -> (a b)")[None, :]), dma=True)
        p.step(nc.sync, lambda erow=erow: nc.sync.dma_start(
            out=o_pair[erow:erow + CAP, :].rearrange("(a b) c -> a (b c)", b=2),
            in_=pair_i[erow:erow + CAP, :].rearrange("(a b) c -> a (b c)", b=2)), dma=True)
        p.step(nc.vector, lambda: nc.vector.reduce_sum(r11[:, :], w_row[:, :], mybir.AxisListType.X))
        p.step(nc.vector, lambda: nc.vector.tensor_single_scalar(r11[:, :], r11[:, :], 1e-9, ALU.add))
        p.step(nc.vector, lambda: nc.vector.reciprocal(r11[:, :], r11[:, :]))
        p.step(nc.vector, lambda: nc.vector.tensor_scalar(
            wn_b[:, :], w_row[:, :], r11[0:1, 0:1], None, ALU.mult))
        p.step(nc.tensor, lambda: nc.tensor.matmul(
            ps_g1[:, :], ones_row_b[0:1, :], wn_b[:, :], start=True, stop=True))
        p.step(nc.vector, lambda: nc.vector.tensor_copy(w_bc[:, :], ps_g1[:, :]))
        for g in range(2):
            for j in range(8):
                p.step(nc.tensor, lambda g=g, j=j: nc.tensor.transpose(
                    ps_tp[:, :], xe_tok[g][:, j * 128:(j + 1) * 128], ident["id_b"][:, :]))
                p.step(nc.vector, lambda g=g, j=j: nc.vector.tensor_copy(
                    xe_f[j][:, g * 128:(g + 1) * 128], ps_tp[:, :]))
        for i in range(16):
            for j in range(8):
                p.step(nc.gpsimd, lambda e=e, i=i, j=j: nc.gpsimd.dma_start(
                    out=blk[:, :], in_=w1_d[e * D + j * 128:e * D + (j + 1) * 128,
                                            i * 128:(i + 1) * 128]), dma=True)
                p.step(nc.tensor, lambda i=i, j=j: nc.tensor.matmul(
                    ps_g1[:, :], blk[:, :], xe_f[j][:, :], start=(j == 0), stop=(j == 7)))
            for j in range(8):
                p.step(nc.gpsimd, lambda e=e, i=i, j=j: nc.gpsimd.dma_start(
                    out=blk[:, :], in_=w2_d[e * D + j * 128:e * D + (j + 1) * 128,
                                            i * 128:(i + 1) * 128]), dma=True)
                p.step(nc.tensor, lambda i=i, j=j: nc.tensor.matmul(
                    ps_g2[:, :], blk[:, :], xe_f[j][:, :], start=(j == 0), stop=(j == 7)))
            p.step(nc.scalar, lambda: nc.scalar.activation(g_b[:, :], ps_g1[:, :], AF.Silu))
            p.step(nc.vector, lambda i=i: nc.vector.tensor_mul(gv[i][:, :], g_b[:, :], ps_g2[:, :]))
        for d in range(8):
            for i in range(16):
                p.step(nc.gpsimd, lambda e=e, i=i, d=d: nc.gpsimd.dma_start(
                    out=blk[:, :], in_=w3_d[e * 2048 + i * 128:e * 2048 + (i + 1) * 128,
                                            d * 128:(d + 1) * 128]), dma=True)
                p.step(nc.tensor, lambda i=i, d=d: nc.tensor.matmul(
                    ps_g1[:, :], blk[:, :], gv[i][:, :], start=(i == 0), stop=(i == 15)))
            p.step(nc.vector, lambda: nc.vector.tensor_mul(wt_f[:, :], ps_g1[:, :], w_bc[:, :]))
            for g in range(2):
                p.step(nc.tensor, lambda g=g: nc.tensor.transpose(
                    ps_tf[:, :], wt_f[:, g * 128:(g + 1) * 128], ident["id_f"][:, :]))
                p.step(nc.vector, lambda g=g, d=d: nc.vector.tensor_copy(
                    wout_tok[g][:, d * 128:(d + 1) * 128], ps_tf[:, :]))
        for g in range(2):
            p.step(nc.sync, lambda g=g, erow=erow: nc.sync.dma_start(
                out=o_wout[erow + g * 128:erow + (g + 1) * 128, :], in_=wout_tok[g][:, :]), dma=True)
    p.finish()
    return nc


def build_comb():
    nc = bass.Bass("TRN2", target_bir_lowering=False)
    p = P(nc)
    x2_d = nc.dram_tensor("x2", [D, TC], F32, kind="ExternalInput").ap()
    pair_d = nc.dram_tensor("pair_all", [NE * CAP, 2], F32, kind="ExternalInput").ap()
    wout_d = nc.dram_tensor("wout_all", [NE * CAP, D], F32, kind="ExternalInput").ap()
    off_d = nc.dram_tensor("off", [128, 1], F32, kind="ExternalInput").ap()
    idm = nc.dram_tensor("idm", [128, 128], F32, kind="ExternalInput").ap()
    y = nc.dram_tensor("y", [TC, D], F32, kind="ExternalOutput").ap()
    yacc = nc.dram_tensor("yacc", [TC, D], F32, kind="Internal").ap()

    ident = make_identities(nc, p, idm)
    x2c = [p.sb([128, TC], F32) for _ in range(8)]
    for c in range(8):
        p.step(nc.sync, lambda c=c: nc.sync.dma_start(
            out=x2c[c][:, :], in_=x2_d[c * 128:(c + 1) * 128, :]), dma=True)
    ytile = p.sb([128, D], F32)
    ps_tf = p.ps([128, 128], F32)
    off_t = p.sb([128, 1], F32)
    big_c = p.sb([128, 1], F32)
    idxf = p.sb([128, 1], F32)
    vcol = p.sb([128, 1], F32)
    vcol_i = p.sb([128, 1], I32)
    idxi = p.sb([128, 1], I32)
    wrows = p.sb([128, D], F32)
    p.step(nc.sync, lambda: nc.sync.dma_start(out=off_t[:, :], in_=off_d), dma=True)
    p.step(nc.vector, lambda: nc.vector.memset(big_c[:, :], 1e9))
    for tb in range(16):
        for c in range(8):
            p.step(nc.tensor, lambda tb=tb, c=c: nc.tensor.transpose(
                ps_tf[:, :], x2c[c][:, tb * 128:(tb + 1) * 128], ident["id_f"][:, :]))
            p.step(nc.vector, lambda c=c: nc.vector.tensor_copy(
                ytile[:, c * 128:(c + 1) * 128], ps_tf[:, :]))
        p.step(nc.sync, lambda tb=tb: nc.sync.dma_start(
            out=yacc[tb * 128:(tb + 1) * 128, :], in_=ytile[:, :]), dma=True)
    for e in range(NE):
        for g in range(2):
            with nc.allow_non_contiguous_dma(reason="tiny 128-elem strided index column"):
                p.step(nc.sync, lambda e=e, g=g: nc.sync.dma_start(
                    out=idxf[:, :], in_=pair_d[e * CAP + g * 128:e * CAP + (g + 1) * 128, 0:1]), dma=True)
            p.step(nc.vector, lambda: nc.vector.tensor_scalar(
                idxf[:, :], idxf[:, :], off_t[:, 0:1], None, ALU.subtract))
            p.step(nc.vector, lambda: nc.vector.tensor_single_scalar(
                vcol[:, :], idxf[:, :], 0.0, ALU.is_lt))
            p.step(nc.vector, lambda: nc.vector.tensor_copy(vcol_i[:, :], vcol[:, :]))
            p.step(nc.vector, lambda: nc.vector.select(
                idxf[:, :], vcol_i[:, :], big_c[:, :], idxf[:, :]))
            p.step(nc.vector, lambda: nc.vector.tensor_copy(idxi[:, :], idxf[:, :]))
            p.step(nc.sync, lambda e=e, g=g: nc.sync.dma_start(
                out=wrows[:, :], in_=wout_d[e * CAP + g * 128:e * CAP + (g + 1) * 128, :]), dma=True)
            p.step(nc.gpsimd, lambda: nc.gpsimd.indirect_dma_start(
                yacc, IndirectOffsetOnAxis(idxi[:, 0:1], axis=0),
                wrows[:, :], None,
                bounds_check=TC - 1, oob_is_err=False, compute_op=ALU.add), dma=True)
    ycp = p.sb([128, D], F32)
    for tb in range(16):
        p.step(nc.sync, lambda tb=tb: nc.sync.dma_start(
            out=ycp[:, :], in_=yacc[tb * 128:(tb + 1) * 128, :]), dma=True)
        p.step(nc.sync, lambda tb=tb: nc.sync.dma_start(
            out=y[tb * 128:(tb + 1) * 128, :], in_=ycp[:, :]), dma=True)
    p.finish()
    return nc


def run_moe(inputs, x2_tiles, xn1t, logits):
    idm = np.eye(128, dtype=np.float32)
    ltm = np.triu(np.ones((128, 128), np.float32), 1)
    tv = (np.arange(128)[:, None] + 128 * np.arange(128)[None, :]).astype(np.float32)
    w1 = np.asarray(inputs["L1_ffn_w1"], np.float32)
    w2 = np.asarray(inputs["L1_ffn_w2"], np.float32)
    w3 = np.asarray(inputs["L1_ffn_w3"], np.float32)
    r6 = []
    for g in range(2):
        in6 = []
        for c in range(NCORE):
            e = 2 * c + g
            in6.append(dict(
                lg=np.ascontiguousarray(logits[e].reshape(128, 128).T), xnt=xn1t,
                w1e=np.ascontiguousarray(w1[e]),
                w2e=np.ascontiguousarray(w2[e]),
                w3e=np.ascontiguousarray(w3[e]),
                ltm=ltm, tv=tv, idm=idm,
            ))
        r6.append(_run("moe", build_moe, in6))
    pair_all = np.concatenate([np.asarray(r6[g][c]["o_pair"])
                               for c in range(NCORE) for g in range(2)], axis=0)
    wout_all = np.concatenate([np.asarray(r6[g][c]["o_wout"])
                               for c in range(NCORE) for g in range(2)], axis=0)
    in7 = []
    for c in range(NCORE):
        in7.append(dict(
            x2=x2_tiles[c], pair_all=pair_all, wout_all=wout_all,
            off=np.full((128, 1), float(c * TC), np.float32), idm=idm,
        ))
    r7 = _run("comb", build_comb, in7)
    return [np.asarray(r7[c]["y"]) for c in range(NCORE)]


_NC_CACHE = {}
_L0B_RESULTS = None


_WARMED = set()


def _run(name, builder, in_maps):
    if name not in _NC_CACHE:
        _NC_CACHE[name] = builder()
    if name not in _WARMED:
        run_bass_kernel_spmd(_NC_CACHE[name], in_maps, core_ids=list(range(NCORE)))
        _WARMED.add(name)
    res = run_bass_kernel_spmd(_NC_CACHE[name], in_maps, core_ids=list(range(NCORE)))
    return res.results


def _r128(v, n):
    return np.ascontiguousarray(np.asarray(v, np.float32).reshape(n, 128))


def run_l0(inputs):
    x = np.asarray(inputs["L0_attn_x"])
    xf = np.ascontiguousarray(x.reshape(T, D).T)
    xf_pad = np.concatenate([np.zeros((D, 128), np.float32), xf], axis=1)

    in1 = []
    for c in range(NCORE):
        in1.append(dict(
            xh=np.ascontiguousarray(xf_pad[:, c * TC:(c + 1) * TC + 128]),
            Wk=np.asarray(inputs["L0_attn_Wk"], np.float32),
            Wv=np.asarray(inputs["L0_attn_Wv"], np.float32),
            Wr=np.asarray(inputs["L0_attn_Wr"], np.float32),
            tmix=_r128(inputs["L0_attn_time_mix"], 8),
            td=_r128(inputs["L0_attn_time_decay"], 8),
            tf=_r128(inputs["L0_attn_time_first"], 8),
            idm=np.eye(128, dtype=np.float32),
        ))
    r1 = _run("l0a", build_l0a, in1)

    fin_kv = np.stack([r1[c]["o_finkv"][:, 0] for c in range(NCORE)], axis=1)
    fin_k = np.stack([r1[c]["o_fink"][:, 0] for c in range(NCORE)], axis=1)

    in2 = []
    for c in range(NCORE):
        gaps = np.zeros((1, 8), np.float32)
        maskv = np.zeros((1, 8), np.float32)
        for j in range(c):
            gaps[0, j] = float((c - 1 - j) * TC)
            maskv[0, j] = 1.0
        in2.append(dict(
            xres=np.ascontiguousarray(xf[:, c * TC:(c + 1) * TC]),
            wkv=r1[c]["o_wkv"], wk=r1[c]["o_wk"], sig=r1[c]["o_sig"], xn=r1[c]["o_xn"],
            finkv=np.ascontiguousarray(fin_kv), fink=np.ascontiguousarray(fin_k),
            gaps=gaps, maskv=maskv,
            td=_r128(inputs["L0_attn_time_decay"], 8),
            Wo=np.asarray(inputs["L0_attn_Wo"], np.float32),
            W1=np.asarray(inputs["L0_ffn_W1"], np.float32),
            b1=_r128(inputs["L0_ffn_b1"], 32),
            theta=_r128(inputs["L0_ffn_quantum_phases"], 32),
            qwr=_r128(inputs["L0_ffn_quantum_weights_real"], 32),
            qwi=_r128(inputs["L0_ffn_quantum_weights_imag"], 32),
            W2=np.asarray(inputs["L0_ffn_W2"], np.float32),
            b2=_r128(inputs["L0_ffn_b2"], 8),
            idm=np.eye(128, dtype=np.float32),
            iotad=np.tile(np.arange(1024, dtype=np.float32), (128, 1)),
        ))
    r2 = _run("l0b", build_l0b, in2)
    global _L0B_RESULTS
    _L0B_RESULTS = r2
    return [r2[c]["o_x1"] for c in range(NCORE)]


def kernel(**inputs):
    x1 = run_l0(inputs)
    x2, xn1t, logits, _r3 = run_l1_attn(inputs, x1)
    yts = run_moe(inputs, x2, xn1t, logits)
    out = np.concatenate(yts, axis=0)
    return out.reshape(1, T, D)
